# version 11 (final)
# speedup vs baseline: 1.2842x; 1.0029x over previous
import contextlib
import numpy as np
import concourse.bass as bass
import concourse.mybir as mybir
from concourse.bass_utils import run_bass_kernel_spmd

F32 = mybir.dt.float32
BF16 = mybir.dt.bfloat16
ALU = mybir.AluOpType
AF = mybir.ActivationFunctionType
AX = mybir.AxisListType

NCORES = 8
B, L, D = 4, 4096, 1024
TOK = 2048
NH, HD = 16, 64
LD = 256
KL = 4096
NEG = -1.0e30
EPS = 1e-5

_NAMECTR = [0]


def _u(n):
    _NAMECTR[0] += 1
    return f"{n}__{_NAMECTR[0]}"




class Sched:

    ENGS = ("PE", "ACT", "DVE", "POOL", "SP", "PD")
    DMA_ENGS = ("SP", "PD")

    def __init__(self, nc):
        self.nc = nc
        self.ops = []
        self.last_writer = {}
        self.readers = {}
        self.bar = []

    def barrier(self):
        last = {}
        for i, o in enumerate(self.ops):
            last[o["engine"]] = i
        self.bar = list(last.values())

    def op(self, engine, fn, reads=(), writes=(), short=False, strict=False,
           tag=""):
        idx = len(self.ops)
        deps = set(self.bar)
        for r in reads:
            w = self.last_writer.get(r)
            if w is not None:
                deps.add(w)
        for w in writes:
            lw = self.last_writer.get(w)
            if lw is not None:
                deps.add(lw)
            for rd in self.readers.get(w, ()):
                deps.add(rd)
        for r in reads:
            self.readers.setdefault(r, []).append(idx)
        for w in writes:
            self.last_writer[w] = idx
            self.readers[w] = []
        self.ops.append(dict(engine=engine, fn=fn, deps=deps, short=short,
                             strict=strict, tag=tag, inc=False))
        return idx

    def check_deadlock(self):
        ops = self.ops
        per_eng = {e: [] for e in self.ENGS if e != "PD"}
        for i, o in enumerate(ops):
            e = o["engine"]
            per_eng["POOL" if e == "PD" else e].append(i)
        for e in per_eng:
            per_eng[e].sort()
        pos = {e: 0 for e in per_eng}
        done = [False] * len(ops)
        progress = True
        while progress:
            progress = False
            for e in per_eng:
                while pos[e] < len(per_eng[e]):
                    i = per_eng[e][pos[e]]
                    if all(done[j] for j in ops[i]["rdeps"]):
                        done[i] = True
                        pos[e] += 1
                        progress = True
                    else:
                        break
        stuck = [i for i in range(len(ops)) if not done[i]]
        if stuck:
            i = stuck[0]
            blockers = [j for j in ops[i]["rdeps"] if not done[j]]
            raise RuntimeError(
                f"DEADLOCK: op {i} tag={ops[i]['tag']} eng={ops[i]['engine']} "
                f"blocked on {[(j, ops[j]['tag'], ops[j]['engine']) for j in blockers[:5]]}; "
                f"{len(stuck)} ops stuck")

    def emit(self):
        ops = self.ops
        for i, o in enumerate(ops):
            pruned = set()
            for j in o["deps"]:
                p = ops[j]
                if p["engine"] == o["engine"] and o["engine"] == "PE":
                    continue
                pruned.add(j)
            o["rdeps"] = pruned
            for j in pruned:
                ops[j]["inc"] = True
        for o in ops:
            if o["engine"] in self.DMA_ENGS:
                o["inc"] = True
        per_eng = {e: [] for e in self.ENGS}
        for i, o in enumerate(ops):
            per_eng[o["engine"]].append(i)
        for e, lst in per_eng.items():
            if lst:
                ops[lst[-1]]["inc"] = True
        CAP = 60000
        KROT = 12
        inc_no = {}
        epoch_of = {}
        cnt = {e: 0 for e in self.ENGS}
        epoch = {e: 0 for e in self.ENGS}
        n_epochs = {e: 1 for e in self.ENGS}
        dma_ord = {e: 0 for e in self.DMA_ENGS}
        for i, o in enumerate(ops):
            e = o["engine"]
            if e in self.DMA_ENGS:
                k = dma_ord[e] % KROT
                dma_ord[e] += 1
                epoch_of[i] = k
                n_epochs[e] = min(KROT, max(n_epochs[e], k + 1))
                cnt_key = (e, k)
                inc_no[i] = 16 * (dma_ord[e] // KROT + (1 if k < dma_ord[e] % KROT else 0))
            else:
                if o["inc"]:
                    if cnt[e] + 1 > CAP:
                        epoch[e] += 1
                        n_epochs[e] = epoch[e] + 1
                        cnt[e] = 0
                    cnt[e] += 1
                inc_no[i] = cnt[e]
                epoch_of[i] = epoch[e]
        class_ord = {}
        counts = {}
        for i, o in enumerate(ops):
            e = o["engine"]
            if e in self.DMA_ENGS:
                k = epoch_of[i]
                c = counts.get((e, k), 0) + 1
                counts[(e, k)] = c
                inc_no[i] = 16 * c
        self.final_counts = {}
        for e in self.ENGS:
            for k in range(n_epochs[e]):
                self.final_counts[(e, k)] = 0
        cnt2 = {}
        for i, o in enumerate(ops):
            e = o["engine"]
            if o["inc"] or e in self.DMA_ENGS:
                key = (e, epoch_of[i])
                self.final_counts[key] = inc_no[i]
        self.check_deadlock()
        nc = self.nc
        sems = {}
        sem_h = {}
        for e in self.ENGS:
            for k in range(n_epochs[e]):
                s = nc.semaphore(f"sem_{e}_{k}")
                sems[(e, k)] = s
                sem_h[(e, k)] = s.__enter__()

        waited = set()
        for o in ops:
            waited.update(o["rdeps"])

        def run_engine(eng_name, eng_obj):
            water = {}
            if eng_name == "GPMERGED":
                idxs = sorted(per_eng["POOL"] + per_eng["PD"])
            else:
                idxs = per_eng[eng_name]
            for i in idxs:
                o = ops[i]
                eng_name = o["engine"]
                need = {}
                if eng_name in self.DMA_ENGS:
                    key = (eng_name, epoch_of[i])
                    prev = inc_no[i] - 16
                    if prev > 0:
                        need[key] = prev
                for j in o["rdeps"]:
                    je = ops[j]["engine"]
                    pk = (je, epoch_of[j])
                    need[pk] = max(need.get(pk, 0), inc_no[j])
                    if je not in self.DMA_ENGS:
                        for kk in range(epoch_of[j]):
                            pk2 = (je, kk)
                            need[pk2] = max(need.get(pk2, 0),
                                            self.final_counts[pk2])
                for pk, v in sorted(need.items()):
                    if v > water.get(pk, 0):
                        eng_obj.wait_ge(sem_h[pk], v)
                        water[pk] = v
                inst = o["fn"](eng_obj)
                if o["inc"] or eng_name in self.DMA_ENGS:
                    amt = 16 if eng_name in self.DMA_ENGS else 1
                    if inst is None:
                        raise RuntimeError(f"op {i} ({o['tag']}) returned no inst")
                    inst.then_inc(sem_h[(eng_name, epoch_of[i])], amt)

        with nc.Block() as blk:
            @blk.tensor
            def _(t):
                run_engine("PE", t)

            @blk.scalar
            def _(s):
                run_engine("ACT", s)

            @blk.vector
            def _(v):
                run_engine("DVE", v)

            @blk.gpsimd
            def _(g):
                run_engine("GPMERGED", g)

            @blk.sync
            def _(q):
                run_engine("SP", q)
                for key, v in sorted(self.final_counts.items()):
                    if v > 0 and key[0] != "SP":
                        q.wait_ge(sem_h[key], v)
                for k in range(n_epochs["SP"]):
                    if self.final_counts[("SP", k)] > 0:
                        q.wait_ge(sem_h[("SP", k)], self.final_counts[("SP", k)])
        for s in sems.values():
            s.__exit__(None, None, None)


def dch(ap, p=128):
    return ap.rearrange("(n p) t -> n p t", p=p)



def emit_l0_attn(nc, sch, keyx, kbias, band_d, consf_d, consb_d, wq_d,
                 wkv_d, wkup_d, wvup_d, wout_d, att_o, stats_o, stats_scr):
    NDC = D // 128
    NTT = KL // 128
    NQ = TOK // 512

    def kt_count(qt):
        return 20 + 4 * qt

    with contextlib.ExitStack() as stk:
        sb = lambda n, s, d=F32: stk.enter_context(nc.sbuf_tensor(_u(n), s, d))
        pt = lambda n, s: stk.enter_context(nc.psum_tensor(_u(n), s, F32))
        wbuf = sb("wbuf", [128, 14 * 1024], BF16)
        xn = sb("xn", [128, NDC * KL], BF16)
        lat = sb("lat", [128, 2 * KL], BF16)
        yall = sb("yall", [128, 8 * TOK], BF16)
        cf = sb("cf", [128, 258])
        cbb = sb("cbb", [128, 130], BF16)
        kb = sb("kb", [128, 32])
        bandt = sb("bandt", [128, 1152], BF16)
        statc = sb("statc", [128, 96])
        statr = sb("statr", [64, 128])
        wld = sb("wld", [128, 1024])
        att_t = sb("att_t", [128, TOK])
        ps_stat = pt("ps_stat", [128, 512])
        ps_tr = pt("ps_tr", [128, 512])
        ps_a = pt("ps_a", [128, 512])
        ps_b = pt("ps_b", [128, 512])
        ps_y = pt("ps_y", [128, 1024])
        ps_r = pt("ps_r", [128, 512])
        OFF_WQ = 0
        OFF_WKV = 8 * 1024
        OFF_WKUP = OFF_WKV + 8 * 256
        OFF_WVUP = OFF_WKUP + 2 * 1024

        sch.op("SP", lambda q: q.dma_start(out=cf[:], in_=consf_d), writes=[("cf",)])
        sch.op("SP", lambda q: q.dma_start(out=cbb[:], in_=consb_d), writes=[("cb",)])
        sch.op("SP", lambda q: q.dma_start(out=kb[:], in_=kbias), writes=[("kb",)])
        sch.op("SP", lambda q: q.dma_start(out=bandt[:], in_=band_d), writes=[("band",)])

        def load_convert(src_ap, dst_off, width, key):
            for half in range(0, width, 512):
                w = min(512, width - half)
                sl = (half // 512) % 2
                sch.op("SP", lambda q, s=src_ap, h0=half, w=w, sl=sl:
                       q.dma_start(out=wld[:, sl * 512:sl * 512 + w],
                                   in_=s[:, h0:h0 + w]),
                       reads=[("wld_free", sl)], writes=[("wld", sl)],
                       tag=f"ld{key}_{half}")
                sch.op("ACT", lambda e, o=dst_off + half, w=w, sl=sl:
                       e.copy(out=wbuf[:, o:o + w], in_=wld[:, sl * 512:sl * 512 + w]),
                       reads=[("wld", sl)], writes=[key, ("wld_free", sl)],
                       tag=f"cv{key}_{half}")

        for c in range(NDC):
            load_convert(dch(wq_d)[c], OFF_WQ + c * 1024, 1024, ("wq", c))
        for c in range(NDC):
            load_convert(dch(wkv_d)[c], OFF_WKV + c * 256, 256, ("wkv", c))
        for c in range(2):
            load_convert(dch(wkup_d)[c], OFF_WKUP + c * 1024, 1024, ("wkup", c))
        for c in range(2):
            load_convert(dch(wvup_d)[c], OFF_WVUP + c * 1024, 1024, ("wvup", c))

        with contextlib.ExitStack() as stk1:
            xc = stk1.enter_context(nc.sbuf_tensor(_u("xc"), [128, KL], F32))
            stkA = stk1.enter_context(contextlib.ExitStack())
            xbf = stkA.enter_context(nc.sbuf_tensor(_u("xbf"), [128, KL], BF16))
            sqbf = stkA.enter_context(nc.sbuf_tensor(_u("sqbf"), [128, KL], BF16))
            for c in range(NDC):
                sch.op("SP", lambda q, c=c: q.dma_start(
                    out=xc[:], in_=dch(keyx)[c]),
                    reads=[("xc_free",)], writes=[("xc",)], tag=f"xld{c}")
                sch.op("ACT", lambda e: e.copy(out=xbf[:], in_=xc[:]),
                       reads=[("xc",)], writes=[("xbf",)], tag=f"xbf{c}")
                sch.op("ACT", lambda e: e.activation(
                    out=sqbf[:], in_=xc[:], func=AF.Square),
                    reads=[("xc",)], writes=[("sqbf",), ("xc_free",)], tag=f"sq{c}")
                for tt in range(NTT):
                    sch.op("PE", lambda t, c=c, tt=tt: t.matmul(
                        ps_stat[:, tt * 16 + c * 2: tt * 16 + c * 2 + 1],
                        xbf[:, tt * 128:(tt + 1) * 128],
                        cbb[:, 128:129], start=True, stop=True),
                        reads=[("xbf",), ("cb",)], writes=[("pstat", tt, c, 0)],
                        tag=f"stm{c}_{tt}")
                    sch.op("PE", lambda t, c=c, tt=tt: t.matmul(
                        ps_stat[:, tt * 16 + c * 2 + 1: tt * 16 + c * 2 + 2],
                        sqbf[:, tt * 128:(tt + 1) * 128],
                        cbb[:, 128:129], start=True, stop=True),
                        reads=[("sqbf",), ("cb",)], writes=[("pstat", tt, c, 1)],
                        tag=f"stq{c}_{tt}")
            allst = [("pstat", tt, c, s) for tt in range(NTT) for c in range(NDC)
                     for s in range(2)]
            sch.op("DVE", lambda v: v.tensor_reduce(
                out=statc[:, 0:64].rearrange("p (s tt) -> p tt s", s=2),
                in_=ps_stat[:].rearrange("p (tt c s) -> p tt s c", c=8, s=2),
                op=ALU.add, axis=AX.X),
                reads=allst, writes=[("sums",)], short=True, tag="stred")
            sch.op("DVE", lambda v: v.tensor_scalar(
                out=statc[:, 0:64], in0=statc[:, 0:64],
                scalar1=1.0 / D, scalar2=None, op0=ALU.mult),
                reads=[("sums",)], writes=[("mex",)], short=True, strict=True,
                tag="stdiv")
            sch.op("DVE", lambda v: v.tensor_tensor(
                out=statc[:, 64:96], in0=statc[:, 0:32], in1=statc[:, 0:32],
                op=ALU.mult),
                reads=[("mex",)], writes=[("msq",)], short=True, strict=True, tag="msq")
            sch.op("DVE", lambda v: v.tensor_tensor(
                out=statc[:, 64:96], in0=statc[:, 32:64], in1=statc[:, 64:96],
                op=ALU.subtract),
                reads=[("mex",), ("msq",)], writes=[("var",)], short=True,
                strict=True, tag="var")
            sch.op("ACT", lambda e: e.activation(
                out=statc[:, 64:96], in_=statc[:, 64:96], func=AF.Ln, bias=cf[:, 193:194]),
                reads=[("var",), ("cf",)], writes=[("lnv",)], short=True, strict=True,
                tag="lnv")
            sch.op("ACT", lambda e: e.activation(
                out=statc[:, 32:64],
                in_=statc[:, 64:96], func=AF.Exp, scale=-0.5),
                reads=[("lnv",)], writes=[("rstd",)], short=True, strict=True,
                tag="rstd")
            sch.op("PE", lambda t: t.transpose(
                ps_tr[0:64, 0:128], statc[:, 0:64],
                cf[:, 0:128]),
                reads=[("mex",), ("rstd",), ("cf",)], writes=[("pstr",)], tag="sttr")
            sch.op("DVE", lambda v: v.tensor_copy(out=statr[:], in_=ps_tr[0:64, 0:128]),
                   reads=[("pstr",)], writes=[("statr",)], short=True, tag="stcp")
            sch.op("SP", lambda q: q.dma_start(
                out=stats_scr.rearrange("r (a b) -> r a b", b=128)[0],
                in_=statr[0:32, :]),
                reads=[("statr",)], writes=[("sscr0",)])
            sch.op("SP", lambda q: q.dma_start(
                out=stats_scr.rearrange("r (a b) -> r a b", b=128)[1],
                in_=statr[32:64, :]),
                reads=[("statr",)], writes=[("sscr1",)])
            sch.op("SP", lambda q: q.dma_start(
                out=stats_o.rearrange("r (a b) -> r a b", b=128)[0],
                in_=statr[0:32, :]),
                reads=[("statr",)], writes=[("so0",)])
            sch.op("SP", lambda q: q.dma_start(
                out=stats_o.rearrange("r (a b) -> r a b", b=128)[1],
                in_=statr[32:64, :]),
                reads=[("statr",)], writes=[("so1",)])

            sch.barrier()
            stkA.close()
            mrep = stk1.enter_context(nc.sbuf_tensor(_u("mrep"), [128, KL], F32))
            rrep = stk1.enter_context(nc.sbuf_tensor(_u("rrep"), [128, KL], F32))
            if True:
                sch.op("SP", lambda q: q.dma_start(
                    out=mrep[:], in_=stats_scr[0:1, :].broadcast_to((128, KL))),
                    reads=[("sscr0",)], writes=[("mrep",)])
                sch.op("SP", lambda q: q.dma_start(
                    out=rrep[:], in_=stats_scr[1:2, :].broadcast_to((128, KL))),
                    reads=[("sscr1",)], writes=[("rrep",)])
                for c in range(NDC):
                    sch.op("SP", lambda q, c=c: q.dma_start(
                        out=xc[:], in_=dch(keyx)[c]),
                        reads=[("xc_free",)], writes=[("xc",)], tag=f"xl2{c}")
                    sch.op("DVE", lambda v: v.tensor_tensor(
                        out=xc[:], in0=xc[:], in1=mrep[:], op=ALU.subtract),
                        reads=[("xc",), ("mrep",)], writes=[("xc",)],
                        tag=f"sub{c}")
                    sch.op("DVE", lambda v, c=c: v.tensor_tensor(
                        out=xn[:, c * KL:(c + 1) * KL], in0=xc[:], in1=rrep[:],
                        op=ALU.mult),
                        reads=[("xc",), ("rrep",)], writes=[("xn", c), ("xc_free",)],
                        tag=f"mul{c}")

        sch.barrier()
        for lc in range(2):
            for tcol in range(KL // 512):
                ps = ps_a if tcol % 2 == 0 else ps_b
                pk = ("psab", tcol % 2)
                for c in range(NDC):
                    sch.op("PE", lambda t, lc=lc, tcol=tcol, c=c, ps=ps: t.matmul(
                        ps[:], wbuf[:, OFF_WKV + c * 256 + lc * 128:
                                    OFF_WKV + c * 256 + (lc + 1) * 128],
                        xn[:, c * KL + tcol * 512: c * KL + (tcol + 1) * 512],
                        start=(c == 0), stop=(c == NDC - 1)),
                        reads=[("wkv", c), ("xn", c)], writes=[pk],
                        tag=f"lat{lc}_{tcol}_{c}")
                sch.op("ACT", lambda e, lc=lc, tcol=tcol, ps=ps: e.copy(
                    out=lat[:, lc * KL + tcol * 512: lc * KL + (tcol + 1) * 512],
                    in_=ps[:]),
                    reads=[pk], writes=[("lat", lc)], tag=f"latev{lc}_{tcol}")

        sch.barrier()
        with contextlib.ExitStack() as stk2:
            q_h = stk2.enter_context(nc.sbuf_tensor(_u("q_h"), [64, TOK], BF16))
            k_h = stk2.enter_context(nc.sbuf_tensor(_u("k_h"), [64, KL], BF16))
            vt_h = stk2.enter_context(nc.sbuf_tensor(_u("vt_h"), [128, NTT * 65], BF16))
            ptile = stk2.enter_context(nc.sbuf_tensor(_u("ptile"), [128, 1024], BF16))
            srow = stk2.enter_context(nc.sbuf_tensor(_u("srow"), [1, TOK], F32))
            rrow = stk2.enter_context(nc.sbuf_tensor(_u("rrow"), [1, TOK], F32))
            ytmp = stk2.enter_context(nc.sbuf_tensor(_u("ytmp"), [64, 512], F32))
            for h in range(NH):
                for tcol in range(NQ):
                    ps = ps_a if tcol % 2 == 0 else ps_b
                    pk = ("psab", tcol % 2)
                    for c in range(NDC):
                        sch.op("PE", lambda t, h=h, tcol=tcol, c=c, ps=ps: t.matmul(
                            ps[0:64, :],
                            wbuf[:, c * 1024 + h * 64: c * 1024 + (h + 1) * 64],
                            xn[:, c * KL + TOK + tcol * 512:
                               c * KL + TOK + (tcol + 1) * 512],
                            start=(c == 0), stop=(c == NDC - 1)),
                            reads=[("wq", c), ("xn", c)], writes=[pk],
                            tag=f"q{h}_{tcol}_{c}")
                    sch.op("ACT", lambda e, tcol=tcol, ps=ps: e.copy(
                        out=q_h[:, tcol * 512:(tcol + 1) * 512], in_=ps[0:64, :]),
                        reads=[pk], writes=[("qh", tcol)], tag=f"qev{h}_{tcol}")
                for tcol in range(KL // 512):
                    ps = ps_a if tcol % 2 == 0 else ps_b
                    pk = ("psab", tcol % 2)
                    for lc in range(2):
                        sch.op("PE", lambda t, h=h, tcol=tcol, lc=lc, ps=ps: t.matmul(
                            ps[0:64, :],
                            wbuf[:, OFF_WKUP + lc * 1024 + h * 64:
                                 OFF_WKUP + lc * 1024 + (h + 1) * 64],
                            lat[:, lc * KL + tcol * 512: lc * KL + (tcol + 1) * 512],
                            start=(lc == 0), stop=(lc == 1)),
                            reads=[("wkup", lc), ("lat", lc)], writes=[pk],
                            tag=f"k{h}_{tcol}_{lc}")
                    sch.op("ACT", lambda e, tcol=tcol, ps=ps: e.copy(
                        out=k_h[:, tcol * 512:(tcol + 1) * 512], in_=ps[0:64, :]),
                        reads=[pk], writes=[("kh", tcol)], tag=f"kev{h}_{tcol}")
                sch.op("DVE", lambda v: v.memset(
                    vt_h[:].rearrange("p (a b) -> p a b", b=65)[:, :, 64:65], 1.0),
                    writes=[("vtones",)], short=True, tag=f"vto{h}")
                for kt in range(NTT):
                    ps = ps_a if kt % 2 == 0 else ps_b
                    pk = ("psab", kt % 2)
                    for lc in range(2):
                        sch.op("PE", lambda t, h=h, kt=kt, lc=lc, ps=ps: t.matmul(
                            ps[:, 0:64],
                            lat[:, lc * KL + kt * 128: lc * KL + (kt + 1) * 128],
                            wbuf[:, OFF_WVUP + lc * 1024 + h * 64:
                                 OFF_WVUP + lc * 1024 + (h + 1) * 64],
                            start=(lc == 0), stop=(lc == 1)),
                            reads=[("wvup", lc), ("lat", lc)], writes=[pk],
                            tag=f"v{h}_{kt}_{lc}")
                    sch.op("ACT", lambda e, kt=kt, ps=ps: e.copy(
                        out=vt_h[:, kt * 65: kt * 65 + 64], in_=ps[:, 0:64]),
                        reads=[pk], writes=[("vt", kt)], tag=f"vev{h}_{kt}")
                for qt in range(NQ):
                    nkt = kt_count(qt)
                    ysl = qt % 2
                    yk = ("psy", ysl)
                    for kt in range(nkt):
                        ps = ps_a if kt % 2 == 0 else ps_b
                        pk = ("psab", kt % 2)
                        psl = kt % 2
                        sch.op("PE", lambda t, qt=qt, kt=kt, ps=ps: t.matmul(
                            ps[:], k_h[:, kt * 128:(kt + 1) * 128],
                            q_h[:, qt * 512:(qt + 1) * 512], start=True, stop=True),
                            reads=[("kh", kt // 4), ("qh", qt)], writes=[pk],
                            tag=f"s{h}_{qt}_{kt}")
                        sch.op("ACT", lambda e, kt=kt, ps=ps, psl=psl: e.activation(
                            out=ptile[:, psl * 512:(psl + 1) * 512], in_=ps[:],
                            func=AF.Exp, scale=0.125, bias=kb[:, kt:kt + 1]),
                            reads=[pk, ("kb",)], writes=[("pt", psl)],
                            tag=f"e{h}_{qt}_{kt}")
                        delta = 128 * kt - TOK - 512 * qt
                        if delta > -128:
                            sch.op("DVE", lambda v, d=delta, psl=psl: v.tensor_tensor(
                                out=ptile[:, psl * 512:(psl + 1) * 512],
                                in0=ptile[:, psl * 512:(psl + 1) * 512],
                                in1=bandt[:, 512 - d:1024 - d], op=ALU.mult),
                                reads=[("pt", psl), ("band",)], writes=[("pt", psl)],
                                strict=True, tag=f"b{h}_{qt}_{kt}")
                        sch.op("PE", lambda t, kt=kt, psl=psl, ysl=ysl, nkt=nkt:
                               t.matmul(
                            ps_y[0:65, ysl * 512:(ysl + 1) * 512],
                            vt_h[:, kt * 65:(kt + 1) * 65],
                            ptile[:, psl * 512:(psl + 1) * 512],
                            start=(kt == 0), stop=(kt == nkt - 1)),
                            reads=[("pt", psl), ("vt", kt), ("vtones",)],
                            writes=[yk], tag=f"av{h}_{qt}_{kt}")
                    sch.op("ACT", lambda e, qt=qt, ysl=ysl: e.activation(
                        out=srow[:, qt * 512:(qt + 1) * 512],
                        in_=ps_y[64:65, ysl * 512:(ysl + 1) * 512], func=AF.Ln),
                        reads=[yk], writes=[("lns", qt)], short=True,
                        tag=f"ln{h}_{qt}")
                    sch.op("ACT", lambda e, qt=qt: e.activation(
                        out=rrow[:, qt * 512:(qt + 1) * 512],
                        in_=srow[:, qt * 512:(qt + 1) * 512], func=AF.Exp,
                        scale=-1.0),
                        reads=[("lns", qt)], writes=[("rinv", qt)], short=True,
                        strict=True, tag=f"ex{h}_{qt}")
                    sch.op("PE", lambda t, qt=qt: t.matmul(
                        ps_r[0:64, :], cf[0:1, 129:193],
                        rrow[:, qt * 512:(qt + 1) * 512], start=True, stop=True),
                        reads=[("rinv", qt), ("cf",)], writes=[("psr",)],
                        tag=f"rr{h}_{qt}")
                    sch.op("ACT", lambda e, ysl=ysl: e.copy(
                        out=ytmp[:], in_=ps_y[0:64, ysl * 512:(ysl + 1) * 512]),
                        reads=[yk], writes=[("ytmp",)], tag=f"yev{h}_{qt}")
                    sch.op("DVE", lambda v, h=h, qt=qt: v.tensor_tensor(
                        out=yall[(h % 2) * 64:(h % 2) * 64 + 64,
                                 (h // 2) * TOK + qt * 512:
                                 (h // 2) * TOK + (qt + 1) * 512],
                        in0=ytmp[:], in1=ps_r[0:64, :], op=ALU.mult),
                        reads=[("ytmp",), ("psr",)], writes=[("yall", h // 2)],
                        strict=True, tag=f"yn{h}_{qt}")

        sch.barrier()
        for hc in range(NDC):
            load_convert(dch(wout_d)[hc], OFF_WQ + hc * 1024, 1024, ("wq", hc))
        for oc in range(NDC):
            for tcol in range(NQ):
                ps = ps_a if tcol % 2 == 0 else ps_b
                pk = ("psab", tcol % 2)
                for hc in range(NDC):
                    sch.op("PE", lambda t, oc=oc, tcol=tcol, hc=hc, ps=ps: t.matmul(
                        ps[:], wbuf[:, hc * 1024 + oc * 128: hc * 1024 + (oc + 1) * 128],
                        yall[:, hc * TOK + tcol * 512: hc * TOK + (tcol + 1) * 512],
                        start=(hc == 0), stop=(hc == NDC - 1)),
                        reads=[("wq", hc), ("yall", hc)], writes=[pk],
                        tag=f"o{oc}_{tcol}_{hc}")
                sch.op("ACT", lambda e, tcol=tcol, ps=ps: e.copy(
                    out=att_t[:, tcol * 512:(tcol + 1) * 512], in_=ps[:]),
                    reads=[pk], writes=[("att", tcol)], tag=f"oev{oc}_{tcol}")
            sch.op("SP", lambda q, oc=oc: q.dma_start(
                out=dch(att_o)[oc], in_=att_t[:]),
                reads=[("att", 0), ("att", 1), ("att", 2), ("att", 3)],
                writes=[("attdma",)], tag=f"odma{oc}")


def build_l0_attn():
    nc = bass.Bass("TRN2", target_bir_lowering=False, debug=False)
    keyx = nc.dram_tensor("keyx", [D, KL], F32, kind="ExternalInput").ap()
    kbias = nc.dram_tensor("kbias", [128, 32], F32, kind="ExternalInput").ap()
    band_d = nc.dram_tensor("band", [128, 1152], BF16, kind="ExternalInput").ap()
    consf_d = nc.dram_tensor("consf", [128, 258], F32, kind="ExternalInput").ap()
    consb_d = nc.dram_tensor("consb", [128, 130], BF16, kind="ExternalInput").ap()
    wq_d = nc.dram_tensor("wq", [D, NH * HD], F32, kind="ExternalInput").ap()
    wkv_d = nc.dram_tensor("wkv", [D, LD], F32, kind="ExternalInput").ap()
    wkup_d = nc.dram_tensor("wkup", [LD, NH * HD], F32, kind="ExternalInput").ap()
    wvup_d = nc.dram_tensor("wvup", [LD, NH * HD], F32, kind="ExternalInput").ap()
    wout_d = nc.dram_tensor("wout", [NH * HD, D], F32, kind="ExternalInput").ap()
    att_o = nc.dram_tensor("att", [D, TOK], F32, kind="ExternalOutput").ap()
    stats_o = nc.dram_tensor("stats", [2, KL], F32, kind="ExternalOutput").ap()
    stats_scr = nc.dram_tensor("stats_scr", [2, KL], F32, kind="Internal").ap()

    sch = Sched(nc)
    emit_l0_attn(nc, sch, keyx, kbias, band_d, consf_d, consb_d, wq_d, wkv_d,
                 wkup_d, wvup_d, wout_d, att_o, stats_o, stats_scr)
    sch.emit()
    return nc




def _fm(a):
    return np.ascontiguousarray(np.asarray(a, np.float32).T)


def prep_l0_attn_inputs(x_full, ins):
    consf = np.zeros((128, 258), np.float32)
    consf[:, 0:128] = np.eye(128, dtype=np.float32)
    consf[:, 128] = 1.0
    consf[0, 129:193] = 1.0
    consf[:, 193] = EPS
    consb = np.zeros((128, 130), np.float32)
    consb[:, 0:128] = np.eye(128, dtype=np.float32)
    consb[:, 128] = 1.0
    consb = consb.astype(np.dtype("bfloat16") if False else None) if False else consb
    import ml_dtypes
    consb = consb.astype(ml_dtypes.bfloat16)
    band = np.zeros((128, 1152), np.float32)
    cc = np.arange(1152)[None, :] - 512
    band[:, :] = (cc >= np.arange(128)[:, None]).astype(np.float32)
    band = band.astype(ml_dtypes.bfloat16)

    wq = np.asarray(ins["L0_attn_Wq"], np.float32).reshape(D, NH * HD)
    wkv = np.asarray(ins["L0_attn_Wkv_down"], np.float32)
    wkup = np.asarray(ins["L0_attn_Wk_up"], np.float32).reshape(LD, NH * HD)
    wvup = np.asarray(ins["L0_attn_Wv_up"], np.float32).reshape(LD, NH * HD)
    wout = np.asarray(ins["L0_attn_Wout"], np.float32).reshape(NH * HD, D)

    maps = []
    for c in range(NCORES):
        b, hh = c // 2, c % 2
        if hh == 0:
            keyx = _fm(np.roll(np.asarray(x_full[b], np.float32), 2048, axis=0))
        else:
            keyx = _fm(x_full[b])
        kbias = np.zeros((128, 32), np.float32)
        if hh == 0:
            kbias[:, 0:16] = NEG
        maps.append(dict(keyx=keyx, kbias=kbias, band=band, consf=consf,
                         consb=consb, wq=wq, wkv=wkv, wkup=wkup, wvup=wvup,
                         wout=wout))
    return maps


def kernel(**inputs):
    raise NotImplementedError("under construction")



def emit_staircase(sch, nc, v, out, tA, tB, keyp, reads_v, kout, ktA, ktB,
                   eng="DVE"):
    def ts(dst, thr, wkey):
        sch.op(eng, lambda e, dst=dst, thr=thr: e.tensor_scalar(
            out=dst, in0=v, scalar1=thr, scalar2=None, op0=ALU.is_gt),
            reads=list(reads_v), writes=[wkey], tag=f"{keyp}ts{thr}")

    def stt(dst, thr, src, wkey, rkey):
        sch.op(eng, lambda e, dst=dst, thr=thr, src=src: e.scalar_tensor_tensor(
            out=dst, in0=v, scalar=thr, in1=src, op0=ALU.is_gt, op1=ALU.add),
            reads=list(reads_v) + [rkey], writes=[wkey], tag=f"{keyp}st{thr}")

    ts(tA, 0.25, ktA)
    stt(tB, 0.75, tA, ktB, ktA)
    stt(tA, 1.25, tB, ktA, ktB)
    stt(tB, 1.75, tA, ktB, ktA)
    ts(out, 2.5, kout)
    stt(tA, 3.5, out, ktA, kout)
    stt(out, 5.0, tA, kout, ktA)
    stt(tA, 5.0, out, ktA, kout)
    sch.op(eng, lambda e: e.scalar_tensor_tensor(
        out=out, in0=tB, scalar=0.5, in1=tA, op0=ALU.mult, op1=ALU.add),
        reads=[ktB, ktA], writes=[kout], tag=f"{keyp}comb")
    return kout



def emit_wq_quant(nc, sch, wg_d, wu_d, wd_d, identb_d, wgT_o, wuT_o, wdT_o):
    with contextlib.ExitStack() as stk:
        sb = lambda n, s, d=F32: stk.enter_context(nc.sbuf_tensor(_u(n), s, d))
        identb = sb("identb_s", [128, 128], BF16)
        wt = sb("wt", [128, 2816])
        aw = sb("aw", [128, 2816])
        tA = sb("tA", [128, 2816])
        tB = sb("tB", [128, 2816])
        qn = sb("qn", [128, 2816])
        sg = sb("sg", [128, 2816])
        qbf = sb("qbf", [128, 2816], BF16)
        bm = sb("bm", [128, 176])
        sc_ = sb("sc", [128, 176])
        ri = sb("ri", [128, 176])
        ps_t = stk.enter_context(nc.psum_tensor(_u("ps_t"), [128, 2048], BF16))
        qout = sb("qout", [128, 512], BF16)
        sch.op("SP", lambda q: q.dma_start(out=identb[:], in_=identb_d),
               writes=[("id",)])

        def quant_tile(src_ap, rows, width, out_ap, out_row_off, tagp):
            nb = width // 16
            sch.op("SP", lambda q: q.dma_start(out=wt[0:rows, 0:width], in_=src_ap),
                   reads=[("busy",)], writes=[("wt",)], tag=f"{tagp}ld")
            sch.op("ACT", lambda e: e.activation(
                out=aw[0:rows, 0:width], in_=wt[0:rows, 0:width], func=AF.Abs),
                reads=[("wt",)], writes=[("aw",)], tag=f"{tagp}abs")
            sch.op("ACT", lambda e: e.activation(
                out=sg[0:rows, 0:width], in_=wt[0:rows, 0:width], func=AF.Sign),
                reads=[("wt",)], writes=[("sg",)], tag=f"{tagp}sgn")
            sch.op("DVE", lambda v: v.tensor_reduce(
                out=bm[0:rows, 0:nb],
                in_=aw[0:rows, 0:width].rearrange("p (b s) -> p b s", s=16),
                op=ALU.max, axis=AX.X),
                reads=[("aw",)], writes=[("bm",)], short=True, tag=f"{tagp}bm")
            sch.op("DVE", lambda v: v.tensor_scalar(
                out=sc_[0:rows, 0:nb], in0=bm[0:rows, 0:nb],
                scalar1=1.0 / 6.0, scalar2=1e-12, op0=ALU.mult, op1=ALU.max),
                reads=[("bm",)], writes=[("sc",)], short=True, strict=True,
                tag=f"{tagp}sc")
            sch.op("DVE", lambda v: v.reciprocal(
                out=ri[0:rows, 0:nb], in_=sc_[0:rows, 0:nb]),
                reads=[("sc",)], writes=[("ri",)], short=True, strict=True,
                tag=f"{tagp}ri")
            sch.op("DVE", lambda v: v.tensor_tensor(
                out=aw[0:rows, 0:width].rearrange("p (b s) -> p b s", s=16),
                in0=aw[0:rows, 0:width].rearrange("p (b s) -> p b s", s=16),
                in1=ri[0:rows, 0:nb].unsqueeze(2).broadcast_to((rows, nb, 16)),
                op=ALU.mult),
                reads=[("aw",), ("ri",)], writes=[("aw",)], strict=True,
                tag=f"{tagp}v")
            emit_staircase(sch, nc, aw[0:rows, 0:width], qn[0:rows, 0:width],
                           tA[0:rows, 0:width], tB[0:rows, 0:width],
                           (tagp,), [("aw",)], ("qn",), ("tA",), ("tB",))
            sch.op("DVE", lambda v: v.tensor_tensor(
                out=qn[0:rows, 0:width], in0=qn[0:rows, 0:width],
                in1=sg[0:rows, 0:width], op=ALU.mult),
                reads=[("qn",), ("sg",)], writes=[("qn",)], tag=f"{tagp}qs")
            sch.op("DVE", lambda v: v.tensor_tensor(
                out=qbf[0:rows, 0:width].rearrange("p (b s) -> p b s", s=16),
                in0=qn[0:rows, 0:width].rearrange("p (b s) -> p b s", s=16),
                in1=sc_[0:rows, 0:nb].unsqueeze(2).broadcast_to((rows, nb, 16)),
                op=ALU.mult),
                reads=[("qn",), ("sc",)], writes=[("qbf",)], strict=True,
                tag=f"{tagp}q")
            _steps = globals().get("WQ_STEPS", "c")
            if _steps == "a":
                return
            for cc in range(width // 128):
                psl = cc % 2
                sch.op("PE", lambda t, cc=cc, psl=psl: t.transpose(
                    ps_t[:, psl * 1024:psl * 1024 + rows],
                    qbf[0:rows, cc * 128:(cc + 1) * 128],
                    identb[0:rows, 0:rows]),
                    reads=[("qbf",), ("id",)], writes=[("pst", psl)],
                    tag=f"{tagp}tr{cc}")
                sch.op("DVE", lambda v, cc=cc, psl=psl: v.tensor_copy(
                    out=qout[:, psl * 128:psl * 128 + rows],
                    in_=ps_t[0:128, psl * 1024:psl * 1024 + rows]),
                    reads=[("pst", psl)], writes=[("qo", psl)], tag=f"{tagp}ev{cc}")
                if _steps != "b":
                    sch.op("SP", lambda q, cc=cc, psl=psl: q.dma_start(
                        out=out_ap.rearrange("(n p) r -> n p r", p=128)[cc]
                        [:, out_row_off:out_row_off + rows],
                        in_=qout[:, psl * 128:psl * 128 + rows]),
                        reads=[("qo", psl)], writes=[("busy",)], tag=f"{tagp}st{cc}")

        _sub = globals().get("WQ_SUBSET", 99)
        _n = 0
        for i, (rows, off) in enumerate([(128, 0), (128, 128), (96, 256)]):
            if _n < _sub:
                quant_tile(wg_d[off:off + rows, :], rows, 1024, wgT_o, off, f"g{i}")
            _n += 1
        for i, (rows, off) in enumerate([(128, 0), (128, 128), (96, 256)]):
            if _n < _sub:
                quant_tile(wu_d[off:off + rows, :], rows, 1024, wuT_o, off, f"u{i}")
            _n += 1
        if _n < _sub:
            quant_tile(wd_d, 128, 2816, wdT_o, 0, "d0")

def build_wq_quant():
    nc = bass.Bass("TRN2", target_bir_lowering=False, debug=False)
    wg_d = nc.dram_tensor("wg", [352, 1024], F32, kind="ExternalInput").ap()
    wu_d = nc.dram_tensor("wu", [352, 1024], F32, kind="ExternalInput").ap()
    wd_d = nc.dram_tensor("wd", [128, 2816], F32, kind="ExternalInput").ap()
    identb_d = nc.dram_tensor("identb", [128, 128], BF16, kind="ExternalInput").ap()
    wgT_o = nc.dram_tensor("wgT", [1024, 352], BF16, kind="ExternalOutput").ap()
    wuT_o = nc.dram_tensor("wuT", [1024, 352], BF16, kind="ExternalOutput").ap()
    wdT_o = nc.dram_tensor("wdT", [2816, 128], BF16, kind="ExternalOutput").ap()

    sch = Sched(nc)
    emit_wq_quant(nc, sch, wg_d, wu_d, wd_d, identb_d, wgT_o, wuT_o, wdT_o)
    sch.emit()
    return nc




def emit_l0_ffn(nc, sch, x_d, att_d, st_d, wgT_ld, wdT_ld, identf_d, identb_d,
                bp_d, x1_o, hq_scr):
    II = 2816
    NI = II // 128
    NB = II // 16
    with contextlib.ExitStack() as stk:
        sb = lambda n, s, d=F32: stk.enter_context(nc.sbuf_tensor(_u(n), s, d))
        identf = sb("identf_s", [128, 128])
        identb = sb("identb_s", [128, 128], BF16)
        bp = sb("bp", [64, 1024])
        ps_tr = stk.enter_context(nc.psum_tensor(_u("ps_tr"), [128, 1024], F32))
        ps_rep = stk.enter_context(nc.psum_tensor(_u("ps_rep"), [128, 1024], F32))
        ps_m0 = stk.enter_context(nc.psum_tensor(_u("ps_m0"), [128, 512], F32))
        ps_m1 = stk.enter_context(nc.psum_tensor(_u("ps_m1"), [128, 512], F32))
        ps_tb = stk.enter_context(nc.psum_tensor(_u("ps_tb"), [128, 2048], BF16))
        sch.op("SP", lambda q: q.dma_start(out=identf[:], in_=identf_d), writes=[("idf",)])
        sch.op("SP", lambda q: q.dma_start(out=identb[:], in_=identb_d), writes=[("idb",)])
        sch.op("SP", lambda q: q.dma_start(out=bp[:], in_=bp_d), writes=[("bp",)])

        with contextlib.ExitStack() as s1:
            wg_sb = s1.enter_context(nc.sbuf_tensor(_u("wg_sb"), [128, 8 * II], BF16))
            wu_sb = s1.enter_context(nc.sbuf_tensor(_u("wu_sb"), [128, 8 * II], BF16))
            C1 = s1.enter_context(nc.sbuf_tensor(_u("C1"), [128, II], F32))
            C2 = s1.enter_context(nc.sbuf_tensor(_u("C2"), [128, II], F32))
            C3 = s1.enter_context(nc.sbuf_tensor(_u("C3"), [128, II], F32))
            C4 = s1.enter_context(nc.sbuf_tensor(_u("C4"), [128, II], F32))
            C5 = s1.enter_context(nc.sbuf_tensor(_u("C5"), [128, II], F32))
            xt = s1.enter_context(nc.sbuf_tensor(_u("xt"), [128, 1024], F32))
            xq = s1.enter_context(nc.sbuf_tensor(_u("xq"), [128, 1024], BF16))
            hqt = s1.enter_context(nc.sbuf_tensor(_u("hqt"), [128, II], BF16))
            mr = s1.enter_context(nc.sbuf_tensor(_u("mr"), [128, 512], F32))
            rr = s1.enter_context(nc.sbuf_tensor(_u("rr"), [128, 512], F32))
            bm = s1.enter_context(nc.sbuf_tensor(_u("bm"), [128, NB], F32))
            scb = s1.enter_context(nc.sbuf_tensor(_u("scb"), [128, NB], F32))
            rib = s1.enter_context(nc.sbuf_tensor(_u("rib"), [128, NB], F32))
            rcq = s1.enter_context(nc.sbuf_tensor(_u("rcq"), [64, 256], F32))
            qev = s1.enter_context(nc.sbuf_tensor(_u("qev"), [128, 2048], BF16))
            wgT_ld(wg_sb, wu_sb)
            for sup in range(4):
                sch.op("SP", lambda q, sup=sup: q.dma_start(
                    out=mr[:], in_=st_d[0:1, sup * 512:(sup + 1) * 512]
                    .broadcast_to((128, 512))),
                    reads=[("mrf",)], writes=[("mr",)], tag=f"mr{sup}")
                sch.op("SP", lambda q, sup=sup: q.dma_start(
                    out=rr[:], in_=st_d[1:2, sup * 512:(sup + 1) * 512]
                    .broadcast_to((128, 512))),
                    reads=[("rrf",)], writes=[("rr",)], tag=f"rr{sup}")
                for sub in range(4):
                    ti = sup * 4 + sub
                    t0 = ti * 128
                    sch.op("SP", lambda q, t0=t0: q.dma_start(
                        out=xt[:].rearrange("p (c t) -> p c t", t=128),
                        in_=x_d.rearrange("(c p) t -> p c t", p=128)
                        [:, :, t0:t0 + 128]),
                        reads=[("xtf",)], writes=[("xt",)], tag=f"xld{ti}")
                    sch.op("DVE", lambda v, sub=sub: v.tensor_tensor(
                        out=C2[:, 0:1024].rearrange("p (c t) -> p c t", t=128),
                        in0=xt[:].rearrange("p (c t) -> p c t", t=128),
                        in1=mr[:, sub * 128:(sub + 1) * 128].unsqueeze(1)
                        .broadcast_to((128, 8, 128)), op=ALU.subtract),
                        reads=[("xt",), ("mr",)], writes=[("C2",)], tag=f"xn1_{ti}")
                    sch.op("DVE", lambda v, sub=sub: v.tensor_tensor(
                        out=C2[:, 0:1024].rearrange("p (c t) -> p c t", t=128),
                        in0=C2[:, 0:1024].rearrange("p (c t) -> p c t", t=128),
                        in1=rr[:, sub * 128:(sub + 1) * 128].unsqueeze(1)
                        .broadcast_to((128, 8, 128)), op=ALU.mult),
                        reads=[("C2",), ("rr",)], writes=[("C2",)], tag=f"xn2_{ti}")
                    sch.op("ACT", lambda e: e.activation(
                        out=C1[:, 0:1024], in_=C2[:, 0:1024], func=AF.Abs),
                        reads=[("C2",)], writes=[("C1",)], tag=f"xab{ti}")
                    sch.op("ACT", lambda e: e.activation(
                        out=C3[:, 0:1024], in_=C2[:, 0:1024], func=AF.Sign),
                        reads=[("C2",)], writes=[("C3",)], tag=f"xsg{ti}")
                    for c in range(8):
                        sch.op("PE", lambda t, c=c: t.transpose(
                            ps_tr[:, c * 128:(c + 1) * 128],
                            C1[:, c * 128:(c + 1) * 128], identf[:]),
                            reads=[("C1",), ("idf",)], writes=[("pstr",)],
                            tag=f"xtr{ti}_{c}")
                    sch.op("DVE", lambda v: v.tensor_reduce(
                        out=bm[:, 0:64],
                        in_=ps_tr[:].rearrange("p (b s) -> p b s", s=16),
                        op=ALU.max, axis=AX.X),
                        reads=[("pstr",)], writes=[("bm",)], short=True,
                        tag=f"xbm{ti}")
                    sch.op("DVE", lambda v: v.tensor_scalar(
                        out=scb[:, 0:64], in0=bm[:, 0:64], scalar1=1.0 / 6.0,
                        scalar2=1e-12, op0=ALU.mult, op1=ALU.max),
                        reads=[("bm",)], writes=[("scb",)], short=True,
                        strict=True, tag=f"xsc{ti}")
                    sch.op("DVE", lambda v: v.reciprocal(
                        out=rib[:, 0:64], in_=scb[:, 0:64]),
                        reads=[("scb",)], writes=[("rib",)], short=True,
                        strict=True, tag=f"xri{ti}")
                    sch.op("PE", lambda t: t.transpose(
                        ps_tr[0:64, 0:128], scb[:, 0:64], identf[:]),
                        reads=[("scb",), ("idf",)], writes=[("pstr",)],
                        tag=f"xqt1{ti}")
                    sch.op("PE", lambda t: t.transpose(
                        ps_tr[0:64, 128:256], rib[:, 0:64], identf[:]),
                        reads=[("rib",), ("idf",)], writes=[("pstr",)],
                        tag=f"xqt2{ti}")
                    sch.op("DVE", lambda v: v.tensor_copy(
                        out=rcq[0:64, 0:256], in_=ps_tr[0:64, 0:256]),
                        reads=[("pstr",)], writes=[("rcq",)], short=True,
                        strict=True, tag=f"xqc{ti}")
                    for c in range(8):
                        sch.op("PE", lambda t, c=c: t.matmul(
                            ps_rep[:, c * 128:(c + 1) * 128],
                            bp[:, c * 128:(c + 1) * 128], rcq[0:64, 128:256],
                            start=True, stop=True),
                            reads=[("rcq",), ("bp",)], writes=[("psrep",)],
                            tag=f"xrr{ti}_{c}")
                    sch.op("DVE", lambda v: v.tensor_tensor(
                        out=C1[:, 0:1024], in0=C1[:, 0:1024], in1=ps_rep[:],
                        op=ALU.mult),
                        reads=[("C1",), ("psrep",)], writes=[("C1",)],
                        tag=f"xv{ti}")
                    emit_staircase(sch, nc, C1[:, 0:1024], C2[:, 0:1024],
                                   C4[:, 0:1024], C5[:, 0:1024],
                                   ("xs", ti), [("C1",)], ("C2",), ("C4",), ("C5",))
                    for c in range(8):
                        sch.op("PE", lambda t, c=c: t.matmul(
                            ps_rep[:, c * 128:(c + 1) * 128],
                            bp[:, c * 128:(c + 1) * 128], rcq[0:64, 0:128],
                            start=True, stop=True),
                            reads=[("rcq",), ("bp",)], writes=[("psrep",)],
                            tag=f"xsr{ti}_{c}")
                    sch.op("DVE", lambda v: v.tensor_tensor(
                        out=C2[:, 0:1024], in0=C2[:, 0:1024], in1=C3[:, 0:1024],
                        op=ALU.mult),
                        reads=[("C2",), ("C3",)], writes=[("C2",)],
                        tag=f"xqs{ti}")
                    sch.op("DVE", lambda v: v.tensor_tensor(
                        out=xq[:], in0=C2[:, 0:1024], in1=ps_rep[:], op=ALU.mult),
                        reads=[("C2",), ("psrep",)], writes=[("xq",)],
                        tag=f"xqf{ti}")
                    for nch in range(6):
                        w0 = nch * 512
                        ww = min(512, II - w0)
                        for kc in range(8):
                            sch.op("PE", lambda t, kc=kc, w0=w0, ww=ww: t.matmul(
                                ps_m0[:, 0:ww], xq[:, kc * 128:(kc + 1) * 128],
                                wg_sb[:, kc * II + w0: kc * II + w0 + ww],
                                start=(kc == 0), stop=(kc == 7)),
                                reads=[("xq",), ("wg", kc)], writes=[("psm0",)],
                                tag=f"g{ti}_{nch}_{kc}")
                        sch.op("ACT", lambda e, w0=w0, ww=ww: e.activation(
                            out=C1[:, w0:w0 + ww], in_=ps_m0[:, 0:ww],
                            func=AF.Silu),
                            reads=[("psm0",)], writes=[("C1",)], tag=f"sl{ti}_{nch}")
                        for kc in range(8):
                            sch.op("PE", lambda t, kc=kc, w0=w0, ww=ww: t.matmul(
                                ps_m1[:, 0:ww], xq[:, kc * 128:(kc + 1) * 128],
                                wu_sb[:, kc * II + w0: kc * II + w0 + ww],
                                start=(kc == 0), stop=(kc == 7)),
                                reads=[("xq",), ("wu", kc)], writes=[("psm1",)],
                                tag=f"u{ti}_{nch}_{kc}")
                        sch.op("DVE", lambda v, w0=w0, ww=ww: v.tensor_tensor(
                            out=C2[:, w0:w0 + ww], in0=C1[:, w0:w0 + ww],
                            in1=ps_m1[:, 0:ww], op=ALU.mult),
                            reads=[("C1",), ("psm1",)], writes=[("C2",)],
                            tag=f"h{ti}_{nch}")
                    sch.op("ACT", lambda e: e.activation(
                        out=C1[:], in_=C2[:], func=AF.Abs),
                        reads=[("C2",)], writes=[("C1",)], tag=f"hab{ti}")
                    sch.op("ACT", lambda e: e.activation(
                        out=C3[:], in_=C2[:], func=AF.Sign),
                        reads=[("C2",)], writes=[("C3",)], tag=f"hsg{ti}")
                    sch.op("DVE", lambda v: v.tensor_reduce(
                        out=bm[:], in_=C1[:].rearrange("p (b s) -> p b s", s=16),
                        op=ALU.max, axis=AX.X),
                        reads=[("C1",)], writes=[("bm",)], short=True,
                        tag=f"hbm{ti}")
                    sch.op("DVE", lambda v: v.tensor_scalar(
                        out=scb[:], in0=bm[:], scalar1=1.0 / 6.0, scalar2=1e-12,
                        op0=ALU.mult, op1=ALU.max),
                        reads=[("bm",)], writes=[("scb",)], short=True,
                        strict=True, tag=f"hsc{ti}")
                    sch.op("DVE", lambda v: v.reciprocal(out=rib[:], in_=scb[:]),
                        reads=[("scb",)], writes=[("rib",)], short=True,
                        strict=True, tag=f"hri{ti}")
                    sch.op("DVE", lambda v: v.tensor_tensor(
                        out=C1[:].rearrange("p (b s) -> p b s", s=16),
                        in0=C1[:].rearrange("p (b s) -> p b s", s=16),
                        in1=rib[:].unsqueeze(2).broadcast_to((128, NB, 16)),
                        op=ALU.mult),
                        reads=[("C1",), ("rib",)], writes=[("C1",)],
                        strict=True, tag=f"hv{ti}")
                    emit_staircase(sch, nc, C1[:], C2[:], C4[:], C5[:],
                                   ("hs", ti), [("C1",)], ("C2",), ("C4",), ("C5",))
                    sch.op("DVE", lambda v: v.tensor_tensor(
                        out=C2[:], in0=C2[:], in1=C3[:], op=ALU.mult),
                        reads=[("C2",), ("C3",)], writes=[("C2",)],
                        tag=f"hqs{ti}")
                    sch.op("DVE", lambda v: v.tensor_tensor(
                        out=hqt[:].rearrange("p (b s) -> p b s", s=16),
                        in0=C2[:].rearrange("p (b s) -> p b s", s=16),
                        in1=scb[:].unsqueeze(2).broadcast_to((128, NB, 16)),
                        op=ALU.mult),
                        reads=[("C2",), ("scb",)], writes=[("hqt",)],
                        strict=True, tag=f"hqf{ti}")
                    for i in range(NI):
                        psl = i % 2
                        sch.op("PE", lambda t, i=i, psl=psl: t.transpose(
                            ps_tb[:, psl * 1024:psl * 1024 + 128],
                            hqt[:, i * 128:(i + 1) * 128], identb[:]),
                            reads=[("hqt",), ("idb",)], writes=[("pstb", psl)],
                            tag=f"htr{ti}_{i}")
                        sch.op("DVE", lambda v, i=i, psl=psl: v.tensor_copy(
                            out=qev[:, psl * 1024:psl * 1024 + 128],
                            in_=ps_tb[:, psl * 1024:psl * 1024 + 128]),
                            reads=[("pstb", psl)], writes=[("qe", psl)],
                            tag=f"hev{ti}_{i}")
                        sch.op("SP", lambda q, i=i, psl=psl, t0=t0: q.dma_start(
                            out=dch(hq_scr)[i][:, t0:t0 + 128],
                            in_=qev[:, psl * 1024:psl * 1024 + 128]),
                            reads=[("qe", psl)],
                            writes=[("hscr", i, t0), ("xtf",), ("mrf",), ("rrf",)],
                            tag=f"hst{ti}_{i}")
        sch.barrier()
        with contextlib.ExitStack() as s2:
            wd_sb = s2.enter_context(nc.sbuf_tensor(_u("wd_sb"), [128, NI * 1024], BF16))
            hqf = s2.enter_context(nc.sbuf_tensor(_u("hqf"), [128, NI * 512], BF16))
            xat = s2.enter_context(nc.sbuf_tensor(_u("xat"), [128, 1024], F32))
            x1t = s2.enter_context(nc.sbuf_tensor(_u("x1t"), [128, 512], F32))
            wdT_ld(wd_sb)
            for sup in range(4):
                t0 = sup * 512
                for i in range(NI):
                    sch.op("SP", lambda q, i=i, t0=t0: q.dma_start(
                        out=hqf[:, i * 512:(i + 1) * 512],
                        in_=dch(hq_scr)[i][:, t0:t0 + 512]),
                        reads=[("hqfree", i)], writes=[("hqf", i)],
                        tag=f"hqld{sup}_{i}")
                for oc in range(8):
                    ps = ps_m0 if oc % 2 == 0 else ps_m1
                    pk = ("psm0",) if oc % 2 == 0 else ("psm1",)
                    for i in range(NI):
                        sch.op("PE", lambda t, i=i, oc=oc, ps=ps: t.matmul(
                            ps[:], wd_sb[:, i * 1024 + oc * 128:
                                         i * 1024 + (oc + 1) * 128],
                            hqf[:, i * 512:(i + 1) * 512],
                            start=(i == 0), stop=(i == NI - 1)),
                            reads=[("wd", i), ("hqf", i)], writes=[pk],
                            tag=f"d{sup}_{oc}_{i}")
                    sch.op("SP", lambda q, oc=oc, t0=t0: q.dma_start(
                        out=xat[:, 0:512], in_=dch(x_d)[oc][:, t0:t0 + 512]),
                        reads=[("xatf",)], writes=[("xat1",)], tag=f"c2x{sup}_{oc}")
                    sch.op("SP", lambda q, oc=oc, t0=t0: q.dma_start(
                        out=xat[:, 512:1024], in_=dch(att_d)[oc][:, t0:t0 + 512]),
                        reads=[("xatf",)], writes=[("xat2",)], tag=f"c2a{sup}_{oc}")
                    sch.op("DVE", lambda v: v.tensor_tensor(
                        out=x1t[:], in0=xat[:, 0:512], in1=xat[:, 512:1024],
                        op=ALU.add),
                        reads=[("xat1",), ("xat2",)], writes=[("x1t",), ("xatf",)],
                        tag=f"c2s{sup}_{oc}")
                    sch.op("DVE", lambda v, ps=ps: v.tensor_tensor(
                        out=x1t[:], in0=x1t[:], in1=ps[:], op=ALU.add),
                        reads=[("x1t",), pk], writes=[("x1t",)],
                        strict=True, tag=f"c2f{sup}_{oc}")
                    sch.op("SP", lambda q, oc=oc, t0=t0: q.dma_start(
                        out=dch(x1_o)[oc][:, t0:t0 + 512], in_=x1t[:]),
                        reads=[("x1t",)],
                        writes=[("x1dma",)] + ([("hqfree", i) for i in range(NI)]
                                               if oc == 7 else []),
                        tag=f"c2o{sup}_{oc}")

def build_l0_ffn():
    nc = bass.Bass("TRN2", target_bir_lowering=False, debug=False)
    x_d = nc.dram_tensor("x", [D, TOK], F32, kind="ExternalInput").ap()
    att_d = nc.dram_tensor("att", [D, TOK], F32, kind="ExternalInput").ap()
    st_d = nc.dram_tensor("stats", [2, TOK], F32, kind="ExternalInput").ap()
    wgT_d = nc.dram_tensor("wgT", [1024, 2816], BF16, kind="ExternalInput").ap()
    wuT_d = nc.dram_tensor("wuT", [1024, 2816], BF16, kind="ExternalInput").ap()
    wdT_d = nc.dram_tensor("wdT", [2816, 1024], BF16, kind="ExternalInput").ap()
    identf_d = nc.dram_tensor("identf", [128, 128], F32, kind="ExternalInput").ap()
    identb_d = nc.dram_tensor("identb", [128, 128], BF16, kind="ExternalInput").ap()
    bp_d = nc.dram_tensor("blockpat", [64, 1024], F32, kind="ExternalInput").ap()
    x1_o = nc.dram_tensor("x1", [D, TOK], F32, kind="ExternalOutput").ap()
    hq_scr = nc.dram_tensor("hq_scr", [2816, TOK], BF16, kind="Internal").ap()

    sch = Sched(nc)

    def wgT_ld(sb_wg, sb_wu):
        for c in range(8):
            sch.op("SP", lambda q, c=c: q.dma_start(
                out=sb_wg[:, c * 2816:(c + 1) * 2816], in_=dch(wgT_d)[c]),
                writes=[("wg", c)], tag=f"wgld{c}")
            sch.op("SP", lambda q, c=c: q.dma_start(
                out=sb_wu[:, c * 2816:(c + 1) * 2816], in_=dch(wuT_d)[c]),
                writes=[("wu", c)], tag=f"wuld{c}")

    def wdT_ld(sb_wd):
        for i in range(22):
            sch.op("SP", lambda q, i=i: q.dma_start(
                out=sb_wd[:, i * 1024:(i + 1) * 1024], in_=dch(wdT_d)[i]),
                writes=[("wd", i)], tag=f"wdld{i}")

    emit_l0_ffn(nc, sch, x_d, att_d, st_d, wgT_ld, wdT_ld, identf_d, identb_d,
                bp_d, x1_o, hq_scr)
    sch.emit()
    return nc




def emit_ln_stats(sch, nc, x_d, T, ps_stat, ps_tr, statc, statr, cbb, cf,
                  scr, out_rows, keyp, xc, xbf, sqbf):
    NDC = D // 128
    NTT = T // 128
    for c in range(NDC):
        sch.op("SP", lambda q, c=c: q.dma_start(out=xc[:, 0:T], in_=dch(x_d)[c]),
               reads=[(keyp, "xcf")], writes=[(keyp, "xc")], tag=f"{keyp}xld{c}")
        sch.op("ACT", lambda e: e.copy(out=xbf[:, 0:T], in_=xc[:, 0:T]),
               reads=[(keyp, "xc")], writes=[(keyp, "xbf")], tag=f"{keyp}xbf{c}")
        sch.op("ACT", lambda e: e.activation(out=sqbf[:, 0:T], in_=xc[:, 0:T],
                                             func=AF.Square),
               reads=[(keyp, "xc")], writes=[(keyp, "sq"), (keyp, "xcf")],
               tag=f"{keyp}sq{c}")
        for tt in range(NTT):
            sch.op("PE", lambda t, c=c, tt=tt: t.matmul(
                ps_stat[:, tt * 16 + c * 2: tt * 16 + c * 2 + 1],
                xbf[:, tt * 128:(tt + 1) * 128], cbb[:, 128:129],
                start=True, stop=True),
                reads=[(keyp, "xbf"), ("cb",)], writes=[(keyp, "ps", tt, c, 0)],
                tag=f"{keyp}sm{c}_{tt}")
            sch.op("PE", lambda t, c=c, tt=tt: t.matmul(
                ps_stat[:, tt * 16 + c * 2 + 1: tt * 16 + c * 2 + 2],
                sqbf[:, tt * 128:(tt + 1) * 128], cbb[:, 128:129],
                start=True, stop=True),
                reads=[(keyp, "sq"), ("cb",)], writes=[(keyp, "ps", tt, c, 1)],
                tag=f"{keyp}sq{c}_{tt}")
    allst = [(keyp, "ps", tt, c, s) for tt in range(NTT) for c in range(NDC)
             for s in range(2)]
    nh = NTT
    sch.op("DVE", lambda v: v.tensor_reduce(
        out=statc[:, 0:2 * nh].rearrange("p (s tt) -> p tt s", s=2),
        in_=ps_stat[:, 0:16 * NTT].rearrange("p (tt c s) -> p tt s c", c=8, s=2),
        op=ALU.add, axis=AX.X),
        reads=allst, writes=[(keyp, "sums")], short=True, tag=f"{keyp}red")
    sch.op("DVE", lambda v: v.tensor_scalar(
        out=statc[:, 0:2 * nh], in0=statc[:, 0:2 * nh], scalar1=1.0 / D,
        scalar2=None, op0=ALU.mult),
        reads=[(keyp, "sums")], writes=[(keyp, "mex")], short=True, strict=True,
        tag=f"{keyp}div")
    sch.op("DVE", lambda v: v.tensor_tensor(
        out=statc[:, 2 * nh:3 * nh], in0=statc[:, 0:nh], in1=statc[:, 0:nh],
        op=ALU.mult),
        reads=[(keyp, "mex")], writes=[(keyp, "msq")], short=True, strict=True,
        tag=f"{keyp}msq")
    sch.op("DVE", lambda v: v.tensor_tensor(
        out=statc[:, 2 * nh:3 * nh], in0=statc[:, nh:2 * nh],
        in1=statc[:, 2 * nh:3 * nh], op=ALU.subtract),
        reads=[(keyp, "mex"), (keyp, "msq")], writes=[(keyp, "var")], short=True,
        strict=True, tag=f"{keyp}var")
    sch.op("ACT", lambda e: e.activation(
        out=statc[:, 2 * nh:3 * nh], in_=statc[:, 2 * nh:3 * nh], func=AF.Ln,
        bias=cf[:, 193:194]),
        reads=[(keyp, "var"), ("cf",)], writes=[(keyp, "lnv")], short=True,
        strict=True, tag=f"{keyp}lnv")
    sch.op("ACT", lambda e: e.activation(
        out=statc[:, nh:2 * nh], in_=statc[:, 2 * nh:3 * nh], func=AF.Exp,
        scale=-0.5),
        reads=[(keyp, "lnv")], writes=[(keyp, "rstd")], short=True, strict=True,
        tag=f"{keyp}rstd")
    sch.op("PE", lambda t: t.transpose(
        ps_tr[0:2 * nh, 0:128], statc[:, 0:2 * nh], cf[:, 0:128]),
        reads=[(keyp, "mex"), (keyp, "rstd"), ("cf",)], writes=[(keyp, "pstr")],
        tag=f"{keyp}str")
    sch.op("DVE", lambda v: v.tensor_copy(out=statr[0:2 * nh, :],
                                          in_=ps_tr[0:2 * nh, 0:128]),
           reads=[(keyp, "pstr")], writes=[(keyp, "statr")], short=True,
           tag=f"{keyp}scp")
    outs = [(scr, "scr0", "scr1")]
    if out_rows is not None:
        outs.append((out_rows, "so0", "so1"))
    for dst, k0, k1 in outs:
        sch.op("SP", lambda q, dst=dst: q.dma_start(
            out=dst.rearrange("r (a b) -> r a b", b=128)[0],
            in_=statr[0:nh, :]),
            reads=[(keyp, "statr")], writes=[(keyp, k0)], tag=f"{keyp}d{k0}")
        sch.op("SP", lambda q, dst=dst: q.dma_start(
            out=dst.rearrange("r (a b) -> r a b", b=128)[1],
            in_=statr[nh:2 * nh, :]),
            reads=[(keyp, "statr")], writes=[(keyp, k1)], tag=f"{keyp}d{k1}")


def emit_moa(nc, sch, x1_d, wrT_d, wmhT_d, wmgT_d, kT_d, vf_d, consf_d,
             consb_d, moa_o, xn2_o, st_scr, w_scr):
    NDC = D // 128
    NTT = TOK // 128
    NQ = TOK // 512
    with contextlib.ExitStack() as stk:
        sb = lambda n, s, d=F32: stk.enter_context(nc.sbuf_tensor(_u(n), s, d))
        pt = lambda n, s, d=F32: stk.enter_context(nc.psum_tensor(_u(n), s, d))
        cf = sb("cf", [128, 258])
        cbb = sb("cbb", [128, 130], BF16)
        xn2 = sb("xn2_s", [128, NDC * TOK], BF16)
        qfm = sb("qfm", [128, 8 * TOK], BF16)
        yacc = sb("yacc", [128, 8 * TOK])
        wbuf = sb("wbufD", [128, 8 * 1024], BF16)
        statc = sb("statc", [128, 3 * NTT])
        statr = sb("statr", [2 * NTT, 128])
        wld = sb("wldD", [128, 1024])
        ps_stat = pt("ps_stat", [128, 512])
        ps_tr = pt("ps_tr", [128, 512])
        ps_a = pt("ps_a", [128, 512])
        ps_b = pt("ps_b", [128, 512])
        ps_y0 = pt("ps_y0", [128, 512])
        ps_y1 = pt("ps_y1", [128, 512])
        ps_rl = pt("ps_rl", [128, 512])
        OFF_MH = 0
        OFF_WR = 8 * 1024
        sch.op("SP", lambda q: q.dma_start(out=cf[:], in_=consf_d), writes=[("cf",)])
        sch.op("SP", lambda q: q.dma_start(out=cbb[:], in_=consb_d), writes=[("cb",)])

        def load_convert(src_ap, dst_off, width, key):
            for half in range(0, width, 512):
                w = min(512, width - half)
                sl = (half // 512) % 2
                sch.op("SP", lambda q, s=src_ap, h0=half, w=w, sl=sl:
                       q.dma_start(out=wld[:, sl * 512:sl * 512 + w],
                                   in_=s[:, h0:h0 + w]),
                       reads=[("wldf", sl)], writes=[("wld", sl)],
                       tag=f"ld{key}{half}")
                sch.op("ACT", lambda e, o=dst_off + half, w=w, sl=sl:
                       e.copy(out=wbuf[:, o:o + w], in_=wld[:, sl * 512:sl * 512 + w]),
                       reads=[("wld", sl)], writes=[key, ("wldf", sl)],
                       tag=f"cv{key}{half}")

        with contextlib.ExitStack() as s0:
            xc = s0.enter_context(nc.sbuf_tensor(_u("xcD"), [128, TOK], F32))
            xbf = s0.enter_context(nc.sbuf_tensor(_u("xbfD"), [128, TOK], BF16))
            sqbf = s0.enter_context(nc.sbuf_tensor(_u("sqbfD"), [128, TOK], BF16))
            emit_ln_stats(sch, nc, x1_d, TOK, ps_stat, ps_tr, statc, statr, cbb,
                          cf, st_scr, None, "L", xc, xbf, sqbf)
            mrep = s0.enter_context(nc.sbuf_tensor(_u("mrepD"), [128, TOK], F32))
            rrep = s0.enter_context(nc.sbuf_tensor(_u("rrepD"), [128, TOK], F32))
            sch.op("SP", lambda q: q.dma_start(
                out=mrep[:], in_=st_scr[0:1, :].broadcast_to((128, TOK))),
                reads=[("L", "scr0")], writes=[("mrep",)])
            sch.op("SP", lambda q: q.dma_start(
                out=rrep[:], in_=st_scr[1:2, :].broadcast_to((128, TOK))),
                reads=[("L", "scr1")], writes=[("rrep",)])
            for c in range(NDC):
                sch.op("SP", lambda q, c=c: q.dma_start(
                    out=xc[:], in_=dch(x1_d)[c]),
                    reads=[("L", "xcf")], writes=[("L", "xc")], tag=f"x2ld{c}")
                sch.op("DVE", lambda v: v.tensor_tensor(
                    out=xc[:], in0=xc[:], in1=mrep[:], op=ALU.subtract),
                    reads=[("L", "xc"), ("mrep",)], writes=[("L", "xc")],
                    strict=True, tag=f"x2s{c}")
                sch.op("DVE", lambda v, c=c: v.tensor_tensor(
                    out=xc[:], in0=xc[:], in1=rrep[:], op=ALU.mult),
                    reads=[("L", "xc"), ("rrep",)], writes=[("L", "xc")],
                    strict=True, tag=f"x2m{c}")
                sch.op("SP", lambda q, c=c: q.dma_start(
                    out=dch(xn2_o)[c], in_=xc[:]),
                    reads=[("L", "xc")], writes=[("xn2o", c)], tag=f"x2o{c}")
                sch.op("ACT", lambda e, c=c: e.copy(
                    out=xn2[:, c * TOK:(c + 1) * TOK], in_=xc[:]),
                    reads=[("L", "xc")], writes=[("xn2", c), ("L", "xcf")],
                    tag=f"x2b{c}")
        sch.barrier()
        for c in range(NDC):
            load_convert(dch(wmhT_d)[c], OFF_MH + c * 1024, 1024, ("mh", c))
        with contextlib.ExitStack() as s1:
            wrf = s1.enter_context(nc.sbuf_tensor(_u("wrf"), [128, 8 * 16], F32))
            xn2f = s1.enter_context(nc.sbuf_tensor(_u("xn2f"), [128, 1024], F32))
            for c in range(NDC):
                sch.op("SP", lambda q, c=c: q.dma_start(
                    out=wrf[:, c * 16:(c + 1) * 16], in_=dch(wrT_d)[c]),
                    writes=[("wr", c)], tag=f"wrld{c}")
            rl = s1.enter_context(nc.sbuf_tensor(_u("rl"), [128, 64], F32))
            wsel = s1.enter_context(nc.sbuf_tensor(_u("wsel"), [128, 16], F32))
            wrow = s1.enter_context(nc.sbuf_tensor(_u("wrow"), [16, 128], F32))
            m12 = s1.enter_context(nc.sbuf_tensor(_u("m12"), [128, 2], F32))
            for tt in range(NTT):
                for c in range(NDC):
                    sch.op("SP", lambda q, tt=tt, c=c: q.dma_start(
                        out=xn2f[:, c * 128:(c + 1) * 128],
                        in_=dch(xn2_o)[c][:, tt * 128:(tt + 1) * 128]),
                        reads=[("xn2o", c), ("x2ff", c)], writes=[("x2f", c)],
                        tag=f"x2fl{tt}_{c}")
                for c in range(NDC):
                    sch.op("PE", lambda t, tt=tt, c=c: t.matmul(
                        ps_rl[:, 0:16],
                        xn2f[:, c * 128:(c + 1) * 128],
                        wrf[:, c * 16:(c + 1) * 16],
                        start=(c == 0), stop=(c == NDC - 1)),
                        reads=[("x2f", c), ("wr", c)], writes=[("psrl",)],
                        tag=f"rl{tt}_{c}")
                for c in range(NDC):
                    sch.op("PE", lambda t, c=c: t.engine_nop() if False else None,
                           tag="noop") if False else None
                sch.op("DVE", lambda v: v.tensor_copy(out=rl[:, 0:16],
                                                      in_=ps_rl[:, 0:16]),
                       reads=[("psrl",)],
                       writes=[("rl",)] + [("x2ff", c) for c in range(NDC)],
                       short=True, tag=f"rlc{tt}")
                sch.op("DVE", lambda v: v.reduce_max(
                    out=m12[:, 0:1], in_=rl[:, 0:16], axis=AX.X),
                    reads=[("rl",)], writes=[("m1",)], short=True, strict=True,
                    tag=f"m1_{tt}")
                sch.op("DVE", lambda v: v.tensor_scalar(
                    out=rl[:, 16:32], in0=rl[:, 0:16], scalar1=m12[:, 0:1],
                    scalar2=-65504.0, op0=ALU.is_equal, op1=ALU.mult),
                    reads=[("rl",), ("m1",)], writes=[("kn",)], short=True,
                    strict=True, tag=f"kn{tt}")
                sch.op("DVE", lambda v: v.tensor_tensor(
                    out=rl[:, 16:32], in0=rl[:, 16:32], in1=rl[:, 0:16],
                    op=ALU.add),
                    reads=[("kn",), ("rl",)], writes=[("rl2",)], short=True,
                    strict=True, tag=f"rl2{tt}")
                sch.op("DVE", lambda v: v.reduce_max(
                    out=m12[:, 1:2], in_=rl[:, 16:32], axis=AX.X),
                    reads=[("rl2",)], writes=[("m2",)], short=True, strict=True,
                    tag=f"m2_{tt}")
                sch.op("DVE", lambda v: v.tensor_scalar(
                    out=rl[:, 32:48], in0=rl[:, 0:16], scalar1=m12[:, 1:2],
                    scalar2=None, op0=ALU.is_ge),
                    reads=[("rl",), ("m2",)], writes=[("ind",)], short=True,
                    strict=True, tag=f"ind{tt}")
                sch.op("ACT", lambda e: e.activation(
                    out=rl[:, 48:64], in_=rl[:, 0:16], func=AF.Sigmoid),
                    reads=[("rl",)], writes=[("sig",)], short=True, tag=f"sg{tt}")
                sch.op("DVE", lambda v: v.tensor_tensor(
                    out=wsel[:], in0=rl[:, 32:48], in1=rl[:, 48:64], op=ALU.mult),
                    reads=[("ind",), ("sig",)], writes=[("wsel",)], short=True,
                    strict=True, tag=f"ws{tt}")
                sch.op("PE", lambda t: t.transpose(
                    ps_tr[0:16, 0:128], wsel[:], cf[:, 0:128]),
                    reads=[("wsel",), ("cf",)], writes=[("pswt",)], tag=f"wt{tt}")
                sch.op("DVE", lambda v: v.tensor_copy(
                    out=wrow[:], in_=ps_tr[0:16, 0:128]),
                    reads=[("pswt",)], writes=[("wrow",)], short=True,
                    tag=f"we{tt}")
                sch.op("SP", lambda q, tt=tt: q.dma_start(
                    out=w_scr[:, tt * 128:(tt + 1) * 128], in_=wrow[:]),
                    reads=[("wrow",)], writes=[("wscr", tt)], tag=f"wd{tt}")

        for oc in range(NDC):
            for tcol in range(NQ):
                ps = ps_a if tcol % 2 == 0 else ps_b
                pk = ("psab", tcol % 2)
                for c in range(NDC):
                    sch.op("PE", lambda t, oc=oc, tcol=tcol, c=c, ps=ps: t.matmul(
                        ps[:], wbuf[:, OFF_MH + c * 1024 + oc * 128:
                                    OFF_MH + c * 1024 + (oc + 1) * 128],
                        xn2[:, c * TOK + tcol * 512: c * TOK + (tcol + 1) * 512],
                        start=(c == 0), stop=(c == NDC - 1)),
                        reads=[("mh", c), ("xn2", c)], writes=[pk],
                        tag=f"q{oc}_{tcol}_{c}")
                sch.op("ACT", lambda e, oc=oc, tcol=tcol, ps=ps: e.copy(
                    out=qfm[:, oc * TOK + tcol * 512: oc * TOK + (tcol + 1) * 512],
                    in_=ps[:]),
                    reads=[pk], writes=[("qfm", oc)], tag=f"qe{oc}_{tcol}")

        with contextlib.ExitStack() as s2:
            kvl = s2.enter_context(nc.sbuf_tensor(_u("kvl"), [128, 2048], F32))
            ke = s2.enter_context(nc.sbuf_tensor(_u("ke"), [128, 8 * 256], BF16))
            ve = s2.enter_context(nc.sbuf_tensor(_u("ve"), [128, 8 * 256], BF16))
            wre = s2.enter_context(nc.sbuf_tensor(_u("wre"), [128, TOK], F32))
            acte = s2.enter_context(nc.sbuf_tensor(_u("acte"), [128, 1024], BF16))
            ywt = s2.enter_context(nc.sbuf_tensor(_u("ywt"), [128, 512], F32))
            sch.barrier()
            kT_r = kT_d.rearrange("(h e p) s -> h e p s", h=8, e=16)
            vf_r = vf_d.rearrange("(h e p2 p) s -> h e p2 p s", h=8, e=16, p2=2)
            for e_ in range(16):
                for h in range(8):
                    sch.op("SP", lambda q, h=h, e_=e_: q.dma_start(
                        out=kvl[:, h * 256:(h + 1) * 256], in_=kT_r[h, e_]),
                        writes=[("kvl", h)], tag=f"kl{e_}_{h}")
                for h in range(8):
                    sch.op("ACT", lambda e, h=h: e.copy(
                        out=ke[:, h * 256:(h + 1) * 256],
                        in_=kvl[:, h * 256:(h + 1) * 256]),
                        reads=[("kvl", h)], writes=[("ke", h)],
                        tag=f"kc{e_}_{h}")
                for h in range(8):
                    for p2 in range(2):
                        sch.op("SP", lambda q, h=h, e_=e_, p2=p2: q.dma_start(
                            out=kvl[:, (2 * h + p2) * 128:(2 * h + p2 + 1) * 128],
                            in_=vf_r[h, e_, p2]),
                            writes=[("kvl", h)], tag=f"vl{e_}_{h}_{p2}")
                for h in range(8):
                    for p2 in range(2):
                        sch.op("ACT", lambda e, h=h, p2=p2: e.copy(
                            out=ve[:, (2 * h + p2) * 128:(2 * h + p2 + 1) * 128],
                            in_=kvl[:, (2 * h + p2) * 128:(2 * h + p2 + 1) * 128]),
                            reads=[("kvl", h)], writes=[("ve", h, p2)],
                            tag=f"vc{e_}_{h}_{p2}")
                sch.op("SP", lambda q, e_=e_: q.dma_start(
                    out=wre[:], in_=w_scr[e_:e_ + 1, :].broadcast_to((128, TOK))),
                    reads=[("wre_f",)] + [("wscr", tt) for tt in range(NTT)],
                    writes=[("wre",)], tag=f"wrl{e_}")
                for h in range(8):
                    for tcol in range(NQ):
                        yps = ps_y0 if tcol % 2 == 0 else ps_y1
                        yk = ("psy", tcol % 2)
                        for sc_ in range(2):
                            ps = ps_a if sc_ == 0 else ps_b
                            pk = ("psab", sc_)
                            sch.op("PE", lambda t, h=h, tcol=tcol, sc_=sc_, ps=ps,
                                   e_=e_: t.matmul(
                                ps[:], ke[:, h * 256 + sc_ * 128:
                                          h * 256 + (sc_ + 1) * 128],
                                qfm[:, h * TOK + tcol * 512:
                                    h * TOK + (tcol + 1) * 512],
                                start=True, stop=True),
                                reads=[("ke", h), ("qfm", h)], writes=[pk],
                                tag=f"s{e_}_{h}_{tcol}_{sc_}")
                            sch.op("ACT", lambda e, sc_=sc_, ps=ps: e.activation(
                                out=acte[:, sc_ * 512:(sc_ + 1) * 512], in_=ps[:],
                                func=AF.Gelu),
                                reads=[pk], writes=[("acte", sc_)],
                                tag=f"g{e_}_{h}_{tcol}_{sc_}")
                            sch.op("PE", lambda t, h=h, sc_=sc_, yps=yps, p2=sc_:
                                   t.matmul(
                                yps[:], ve[:, (2 * h + p2) * 128:
                                           (2 * h + p2 + 1) * 128],
                                acte[:, sc_ * 512:(sc_ + 1) * 512],
                                start=(sc_ == 0), stop=(sc_ == 1)),
                                reads=[("acte", sc_), ("ve", h, sc_)],
                                writes=[yk], tag=f"av{e_}_{h}_{tcol}_{sc_}")
                        sch.op("DVE", lambda v, tcol=tcol, yps=yps: v.tensor_tensor(
                            out=ywt[:], in0=yps[:],
                            in1=wre[:, tcol * 512:(tcol + 1) * 512], op=ALU.mult),
                            reads=[yk, ("wre",)], writes=[("ywt",)],
                            tag=f"yw{e_}_{h}_{tcol}")
                        if e_ == 0:
                            sch.op("DVE", lambda v, h=h, tcol=tcol: v.tensor_copy(
                                out=yacc[:, h * TOK + tcol * 512:
                                         h * TOK + (tcol + 1) * 512],
                                in_=ywt[:]),
                                reads=[("ywt",)], writes=[("yacc", h, tcol)],
                                strict=True, tag=f"yi{e_}_{h}_{tcol}")
                        else:
                            sch.op("DVE", lambda v, h=h, tcol=tcol: v.tensor_tensor(
                                out=yacc[:, h * TOK + tcol * 512:
                                         h * TOK + (tcol + 1) * 512],
                                in0=yacc[:, h * TOK + tcol * 512:
                                         h * TOK + (tcol + 1) * 512],
                                in1=ywt[:], op=ALU.add),
                                reads=[("ywt",), ("yacc", h, tcol)],
                                writes=[("yacc", h, tcol)], strict=True,
                                tag=f"ya{e_}_{h}_{tcol}")

        sch.barrier()
        with contextlib.ExitStack() as s3:
            ybf = s3.enter_context(nc.sbuf_tensor(_u("ybf"), [128, 8 * TOK], BF16))
            mot = s3.enter_context(nc.sbuf_tensor(_u("mot"), [128, 512], F32))
            for h in range(8):
                sch.op("ACT", lambda e, h=h: e.copy(
                    out=ybf[:, h * TOK:(h + 1) * TOK],
                    in_=yacc[:, h * TOK:(h + 1) * TOK]),
                    reads=[("yacc", h, 0), ("yacc", h, 1), ("yacc", h, 2),
                           ("yacc", h, 3)],
                    writes=[("ybf", h)], tag=f"yb{h}")
            for c in range(NDC):
                load_convert(dch(wmgT_d)[c], OFF_MH + c * 1024, 1024, ("mh", c))
            for oc in range(NDC):
                for tcol in range(NQ):
                    ps = ps_a if tcol % 2 == 0 else ps_b
                    pk = ("psab", tcol % 2)
                    for c in range(NDC):
                        sch.op("PE", lambda t, oc=oc, tcol=tcol, c=c, ps=ps:
                               t.matmul(
                            ps[:], wbuf[:, OFF_MH + c * 1024 + oc * 128:
                                        OFF_MH + c * 1024 + (oc + 1) * 128],
                            ybf[:, c * TOK + tcol * 512: c * TOK + (tcol + 1) * 512],
                            start=(c == 0), stop=(c == NDC - 1)),
                            reads=[("mh", c), ("ybf", c)], writes=[pk],
                            tag=f"mg{oc}_{tcol}_{c}")
                    sch.op("DVE", lambda v, ps=ps: v.tensor_copy(
                        out=mot[:], in_=ps[:]),
                        reads=[pk], writes=[("mot",)], tag=f"me{oc}_{tcol}")
                    sch.op("SP", lambda q, oc=oc, tcol=tcol: q.dma_start(
                        out=dch(moa_o)[oc][:, tcol * 512:(tcol + 1) * 512],
                        in_=mot[:]),
                        reads=[("mot",)], writes=[("modma",)],
                        tag=f"md{oc}_{tcol}")

def build_moa():
    nc = bass.Bass("TRN2", target_bir_lowering=False, debug=False)
    x1_d = nc.dram_tensor("x1", [D, TOK], F32, kind="ExternalInput").ap()
    wrT_d = nc.dram_tensor("wrT", [D, 16], F32, kind="ExternalInput").ap()
    wmhT_d = nc.dram_tensor("wmhT", [D, D], F32, kind="ExternalInput").ap()
    wmgT_d = nc.dram_tensor("wmgT", [D, D], F32, kind="ExternalInput").ap()
    kT_d = nc.dram_tensor("kT", [8 * 16 * 128, 256], F32, kind="ExternalInput").ap()
    vf_d = nc.dram_tensor("vf", [8 * 16 * 256, 128], F32, kind="ExternalInput").ap()
    consf_d = nc.dram_tensor("consf", [128, 258], F32, kind="ExternalInput").ap()
    consb_d = nc.dram_tensor("consb", [128, 130], BF16, kind="ExternalInput").ap()
    moa_o = nc.dram_tensor("moa", [D, TOK], F32, kind="ExternalOutput").ap()
    xn2_o = nc.dram_tensor("xn2", [D, TOK], F32, kind="ExternalOutput").ap()
    st_scr = nc.dram_tensor("st_scr", [2, TOK], F32, kind="Internal").ap()
    w_scr = nc.dram_tensor("w_scr", [16, TOK], F32, kind="Internal").ap()

    sch = Sched(nc)
    emit_moa(nc, sch, x1_d, wrT_d, wmhT_d, wmgT_d, kT_d, vf_d, consf_d,
             consb_d, moa_o, xn2_o, st_scr, w_scr)
    sch.emit()
    return nc




MAMBA_BUF_NAMES = ['cf', 'cbb', 'wout_sb', 'wdt_sb', 'wx_sb', 'A_sb', 'cw_sb', 'cb_sb', 'bdt_sb', 'dp_sb', 'rmsw_sb', 'hstate', 'usub', 'ygsub', 'winsl', 'brep', 'crep', 'xs_t', 'xc_t', 'delta_t', 'exp_t', 'sz_t', 'dA_t', 'dBx_t', 'h_t', 'hc_t', 'dpxc_t', 'xcbf', 'dt_bf', 'bc_ev', 'tail0', 'tail2', 'rstd_rep', 'xld', 'sqb', 'statc', 'statr', 'wld', 'wldb', 'xout_t', 'ps_dbc', 'ps_a', 'ps_b', 'ps_dl', 'ps_y']
MAMBA_DRAM_NAMES = ['xin_d', 'h0_d', 'win_d', 'rmsw_d', 'convw_d', 'convb_d', 'wx_d', 'wdt_d', 'bdt_d', 'alog_d', 'dp_d', 'wout_d', 'consf_d', 'consb_d', 'xout_o', 'hT_o', 'winb_scr', 'st_scr', 'bc_scr']


def mamba_alloc(nc, stk):
    SUB = 1024
    TOKL = TOK
    sb = lambda n, s, d=F32: stk.enter_context(nc.sbuf_tensor(_u("m_") + n, s, d))
    pt = lambda n, s, d=F32: stk.enter_context(nc.psum_tensor(_u("m_") + n, s, d))
    bufs = {}
    bufs['cf'] = sb("cf", [128, 258])
    bufs['cbb'] = sb("cbb", [128, 130], BF16)
    bufs['wout_sb'] = sb("wout_sb", [128, 16 * 1024], BF16)
    bufs['wdt_sb'] = sb("wdt_sb", [64, 2048], BF16)
    bufs['wx_sb'] = sb("wx_sb", [128, 16 * 96], BF16)
    bufs['A_sb'] = sb("A_sb", [128, 256])
    bufs['cw_sb'] = sb("cw_sb", [128, 64])
    bufs['cb_sb'] = sb("cb_sb", [128, 16])
    bufs['bdt_sb'] = sb("bdt_sb", [128, 16])
    bufs['dp_sb'] = sb("dp_sb", [128, 16])
    bufs['rmsw_sb'] = sb("rmsw_sb", [128, 8])
    bufs['hstate'] = sb("hstate", [128, 256])
    bufs['usub'] = sb("usub", [128, 8 * SUB], BF16)
    bufs['ygsub'] = sb("ygsub", [128, 16 * SUB], BF16)
    bufs['winsl'] = sb("winsl", [128, 2 * 8 * 128], BF16)
    bufs['brep'] = sb("brep", [128, 2 * SUB], BF16)
    bufs['crep'] = sb("crep", [128, 2 * SUB], BF16)
    bufs['xs_t'] = sb("xs_t", [128, 2 * (SUB + 4)])
    bufs['xc_t'] = sb("xc_t", [128, 2 * (SUB)])
    bufs['delta_t'] = sb("delta_t", [128, 2 * (SUB)])
    bufs['exp_t'] = sb("exp_t", [128, SUB])
    bufs['sz_t'] = sb("sz_t", [128, 2 * SUB])
    bufs['dA_t'] = sb("dA_t", [128, 2 * SUB], BF16)
    bufs['dBx_t'] = sb("dBx_t", [128, 2 * SUB], BF16)
    bufs['xcbf'] = sb("xcbf", [128, 2 * SUB], BF16)
    bufs['h_t'] = sb("h_t", [128, 2 * SUB])
    bufs['hc_t'] = sb("hc_t", [128, 2 * SUB], BF16)
    bufs['dpxc_t'] = sb("dpxc_t", [128, 2 * (SUB)], BF16)
    bufs['dt_bf'] = sb("dt_bf", [64, SUB], BF16)
    bufs['bc_ev'] = sb("bc_ev", [32, SUB], BF16)
    bufs['tail0'] = sb("tail0", [128, 64])
    bufs['tail2'] = sb("tail2", [128, 64])
    bufs['rstd_rep'] = sb("rstd_rep", [128, SUB])
    bufs['xld'] = sb("xld", [128, TOK])
    bufs['sqb'] = bufs['usub'][:, 0:TOK]
    bufs['statc'] = sb("statc", [128, 48])
    bufs['statr'] = sb("statr", [16, 128])
    bufs['wld'] = sb("wldM", [128, 512])
    bufs['wldb'] = sb("wldbM", [128, 512], BF16)
    bufs['xout_t'] = sb("xout_t", [128, SUB])
    bufs['ps_dbc'] = pt("ps_dbc", [128, 1024])
    bufs['ps_a'] = pt("ps_a", [128, 512])
    bufs['ps_b'] = pt("ps_b", [128, 512])
    bufs['ps_dl'] = pt("ps_dl", [128, 1024])
    bufs['ps_y'] = pt("ps_y", [128, 1024])
    bufs['hinit'] = stk.enter_context(nc.sbuf_tensor(_u("m_hinit"), [128, 256], F32))
    return bufs


def _defs_mamba():
    pass


def emit_mamba_prep(nc, sch, xin_d, h0_d, win_d, rmsw_d, convw_d, convb_d, wx_d, wdt_d, bdt_d, alog_d, dp_d, wout_d, consf_d, consb_d, xout_o, hT_o, winb_scr, st_scr, bc_scr, cf, cbb, wout_sb, wdt_sb, wx_sb, A_sb, cw_sb, cb_sb, bdt_sb, dp_sb, rmsw_sb, hstate, usub, ygsub, winsl, brep, crep, xs_t, xc_t, delta_t, exp_t, sz_t, dA_t, dBx_t, h_t, hc_t, dpxc_t, xcbf, dt_bf, bc_ev, tail0, tail2, rstd_rep, xld, sqb, statc, statr, wld, wldb, xout_t, ps_dbc, ps_a, ps_b, ps_dl, ps_y, hinit=None):
    NDC = D // 128
    NE = 16
    NSUB = 2
    SUB = 1024
    sch.op("SP", lambda q: q.dma_start(out=cf[:], in_=consf_d), writes=[("cf",)])
    sch.op("SP", lambda q: q.dma_start(out=cbb[:], in_=consb_d), writes=[("cb",)])
    for nm, dst, srcd, key in [("cw", cw_sb, convw_d, "cw"), ("cb2", cb_sb, convb_d, "cvb"),
                               ("bdt", bdt_sb, bdt_d, "bdt"), ("dp", dp_sb, dp_d, "dp"),
                               ("rw", rmsw_sb, rmsw_d, "rw")]:
        sch.op("SP", lambda q, dst=dst, srcd=srcd: q.dma_start(out=dst[:], in_=srcd),
               writes=[(key,)], tag=f"ld{nm}")
    sch.op("SP", lambda q: q.dma_start(out=A_sb[:], in_=alog_d), writes=[("A",)])
    sch.op("ACT", lambda e: e.activation(out=A_sb[:], in_=A_sb[:], func=AF.Exp,
                                         scale=-1.0),
           reads=[("A",)], writes=[("A",)], strict=True, tag="Aneg")
    sch.op("SP", lambda q: q.dma_start(out=A_sb[:], in_=alog_d),
           reads=[("A",)], writes=[("A",)], tag="Areload")
    sch.op("ACT", lambda e: e.activation(out=A_sb[:], in_=A_sb[:], func=AF.Exp),
           reads=[("A",)], writes=[("A",)], strict=True, tag="Aexp")
    sch.op("DVE", lambda v: v.tensor_scalar(
        out=A_sb[:], in0=A_sb[:], scalar1=-1.0, scalar2=None, op0=ALU.mult),
        reads=[("A",)], writes=[("A",)], strict=True, tag="Aneg2")
    for hh_ in range(4):
        sch.op("SP", lambda q, hh_=hh_: q.dma_start(
            out=wld[0:64, :], in_=wdt_d[:, hh_ * 512:(hh_ + 1) * 512]),
            reads=[("wldf",)], writes=[("wldv",)], tag=f"wdt{hh_}")
        sch.op("ACT", lambda e, hh_=hh_: e.copy(
            out=wdt_sb[:, hh_ * 512:(hh_ + 1) * 512], in_=wld[0:64, :]),
            reads=[("wldv",)], writes=[("wdt",), ("wldf",)], tag=f"wdtc{hh_}")
    for hh_ in range(3):
        sch.op("SP", lambda q, hh_=hh_: q.dma_start(
            out=wld[:], in_=wx_d[:, hh_ * 512:(hh_ + 1) * 512]),
            reads=[("wldf",)], writes=[("wldv",)], tag=f"wx{hh_}")
        sch.op("ACT", lambda e, hh_=hh_: e.copy(
            out=wx_sb[:, hh_ * 512:(hh_ + 1) * 512], in_=wld[:]),
            reads=[("wldv",)], writes=[("wx",), ("wldf",)], tag=f"wxc{hh_}")
    for ec in range(NE):
        for hh_ in range(2):
            sch.op("SP", lambda q, ec=ec, hh_=hh_: q.dma_start(
                out=wld[:], in_=dch(wout_d)[ec][:, hh_ * 512:(hh_ + 1) * 512]),
                reads=[("wldf",)], writes=[("wldv",)], tag=f"wo{ec}_{hh_}")
            sch.op("ACT", lambda e, ec=ec, hh_=hh_: e.copy(
                out=wout_sb[:, ec * 1024 + hh_ * 512: ec * 1024 + (hh_ + 1) * 512],
                in_=wld[:]),
                reads=[("wldv",)], writes=[("wo", ec), ("wldf",)],
                tag=f"woc{ec}_{hh_}")
    for c in range(NDC):
        for half in range(8):
            sch.op("SP", lambda q, c=c, half=half: q.dma_start(
                out=wld[:, 0:512],
                in_=dch(win_d)[c][:, half * 512:(half + 1) * 512]),
                reads=[("wldf",)], writes=[("wldv",)], tag=f"wi{c}_{half}")
            sch.op("DVE", lambda v, c=c: v.tensor_scalar(
                out=wldb[:], in0=wld[:, 0:512], scalar1=rmsw_sb[:, c:c + 1],
                scalar2=None, op0=ALU.mult),
                reads=[("wldv",), ("rw",)], writes=[("wldb",)], tag=f"wis{c}_{half}")
            sch.op("SP", lambda q, c=c, half=half: q.dma_start(
                out=dch(winb_scr)[c][:, half * 512:(half + 1) * 512],
                in_=wldb[:]),
                reads=[("wldb",)], writes=[("winb", c, half), ("wldf",)],
                tag=f"wib{c}_{half}")


def emit_mamba_slot(nc, sch, xin_d, h0_d, win_d, rmsw_d, convw_d, convb_d, wx_d, wdt_d, bdt_d, alog_d, dp_d, wout_d, consf_d, consb_d, xout_o, hT_o, winb_scr, st_scr, bc_scr, cf, cbb, wout_sb, wdt_sb, wx_sb, A_sb, cw_sb, cb_sb, bdt_sb, dp_sb, rmsw_sb, hstate, usub, ygsub, winsl, brep, crep, xs_t, xc_t, delta_t, exp_t, sz_t, dA_t, dBx_t, h_t, hc_t, dpxc_t, xcbf, dt_bf, bc_ev, tail0, tail2, rstd_rep, xld, sqb, statc, statr, wld, wldb, xout_t, ps_dbc, ps_a, ps_b, ps_dl, ps_y, st="", hinit=None, kin=None, kout=None):
    NDC = D // 128
    NE = 16
    NSUB = 2
    SUB = 1024
    if kin is None:
        kin = lambda c: ("xinP",)
    if kout is None:
        kout = lambda c: ("xoutP",)
    xs_full, xc_full, dpxc_full = xs_t, xc_t, dpxc_t
    delta_full, exp_full, sz_full = delta_t, exp_t, sz_t
    xs_v = [xs_full[:, 0:SUB + 4], xs_full[:, SUB + 4:2 * (SUB + 4)]]
    xc_v = [xc_full[:, 0:SUB], xc_full[:, SUB:2 * SUB]]
    dpxc_v = [dpxc_full[:, 0:SUB], dpxc_full[:, SUB:2 * SUB]]
    delta_v = [delta_full[:, 0:SUB], delta_full[:, SUB:2 * SUB]]
    exp_v = [exp_full[:, 0:SUB], exp_full[:, 0:SUB]]
    sz_v = [sz_full[:, 0:SUB], sz_full[:, SUB:2 * SUB]]
    xcbf_v = [xcbf[:, 0:SUB], xcbf[:, SUB:2 * SUB]]

    def _p0_body(sub, t0, ec, sl, par, xs_t, xc_t, dpxc_t, kxs, kxc, kxcb):
        for c in range(NDC):
            sch.op("SP", lambda q, c=c, ec=ec, sl=sl: q.dma_start(
                out=winsl[:, (sl * 8 + c) * 128:(sl * 8 + c + 1) * 128],
                in_=dch(winb_scr)[c][:, ec * 128:(ec + 1) * 128]),
                reads=[("wslf", sl)], writes=[("wsl", sl, c)],
                tag=f"p0w{sub}_{ec}_{c}")
        for tc in range(2):
            ps = ps_a if tc == 0 else ps_b
            pk = ("psab", tc)
            for c in range(NDC):
                sch.op("PE", lambda t, c=c, tc=tc, sl=sl, ps=ps: t.matmul(
                    ps[:], winsl[:, (sl * 8 + c) * 128:(sl * 8 + c + 1) * 128],
                    usub[:, c * SUB + tc * 512: c * SUB + (tc + 1) * 512],
                    start=(c == 0), stop=(c == NDC - 1)),
                    reads=[("wsl", sl, c), ("u", c)], writes=[pk],
                    tag=f"p0m{sub}_{ec}_{tc}_{c}")
            sch.op("ACT", lambda e, tc=tc, ps=ps: e.copy(
                out=xs_t[:, 4 + tc * 512: 4 + (tc + 1) * 512], in_=ps[:]),
                reads=[pk], writes=[kxs], tag=f"p0e{sub}_{ec}_{tc}")
        sch.op("DVE", lambda v, ec=ec: v.tensor_copy(
            out=xs_t[:, 1:4], in_=tail0[:, ec * 4:ec * 4 + 3]),
            reads=[("tl0",)], writes=[kxs], short=True,
            tag=f"p0t{sub}_{ec}")
        sch.op("DVE", lambda v, ec=ec: v.tensor_scalar(
            out=xc_t[:], in0=xs_t[:, 1:1 + SUB], scalar1=cw_sb[:, ec * 4:ec * 4 + 1],
            scalar2=None, op0=ALU.mult),
            reads=[kxs, ("cw",)], writes=[kxc], tag=f"cv0{sub}_{ec}")
        for j in range(1, 4):
            sch.op("DVE", lambda v, j=j, ec=ec: v.scalar_tensor_tensor(
                out=xc_t[:], in0=xs_t[:, 1 + j:1 + j + SUB],
                scalar=cw_sb[:, ec * 4 + j:ec * 4 + j + 1], in1=xc_t[:],
                op0=ALU.mult, op1=ALU.add),
                reads=[kxs, ("cw",), kxc], writes=[kxc],
                tag=f"cv{j}{sub}_{ec}")
        sch.op("DVE", lambda v, ec=ec: v.tensor_copy(
            out=tail0[:, ec * 4:ec * 4 + 3], in_=xs_t[:, 1 + SUB:4 + SUB]),
            reads=[kxs], writes=[("tl0",)], short=True,
            tag=f"p0ts{sub}_{ec}")
        sch.op("ACT", lambda e, ec=ec: e.activation(
            out=xc_t[:], in_=xc_t[:], func=AF.Silu,
            bias=cb_sb[:, ec:ec + 1]),
            reads=[kxc, ("cvb",)], writes=[kxc], strict=True,
            tag=f"cs{sub}_{ec}")
        sch.op("ACT", lambda e: e.copy(out=dpxc_t[:], in_=xc_t[:]),
               reads=[kxc], writes=[kxcb], tag=f"xcb{sub}_{ec}")

    def _p0_dbc(sub, ec, dpxc_t, kxcb):
        for half in range(2):
            sch.op("PE", lambda t, ec=ec, half=half, dpxc_t=dpxc_t: t.matmul(
                ps_dbc[0:96, half * 512:(half + 1) * 512],
                wx_sb[:, ec * 96:(ec + 1) * 96],
                dpxc_t[:, half * 512:(half + 1) * 512],
                start=(ec == 0), stop=(ec == NE - 1)),
                reads=[kxcb, ("wx",)], writes=[("psdbc",)],
                tag=f"dbc{sub}_{ec}_{half}")

    def _p2_front(sub, t0, ec, sl, par, xs_t, xc_t, dpxc_t, delta_t, exp_t, sz_t, xcbf_t, kxs, kxc, kxcb, kdlt, kexp, ksz, kxcf):
        for c in range(NDC):
            sch.op("SP", lambda q, c=c, ec=ec, sl=sl: q.dma_start(
                out=winsl[:, (sl * 8 + c) * 128:(sl * 8 + c + 1) * 128],
                in_=dch(winb_scr)[c][:, ec * 128:(ec + 1) * 128]),
                reads=[("wslf", sl)], writes=[("wsl", sl, c)],
                tag=f"p2w{sub}_{ec}_{c}")
        for tc in range(2):
            ps = ps_a if tc == 0 else ps_b
            pk = ("psab", tc)
            for c in range(NDC):
                sch.op("PE", lambda t, c=c, tc=tc, sl=sl, ps=ps: t.matmul(
                    ps[:], winsl[:, (sl * 8 + c) * 128:(sl * 8 + c + 1) * 128],
                    usub[:, c * SUB + tc * 512: c * SUB + (tc + 1) * 512],
                    start=(c == 0), stop=(c == NDC - 1)),
                    reads=[("wsl", sl, c), ("u", c)], writes=[pk],
                    tag=f"p2m{sub}_{ec}_{tc}_{c}")
            sch.op("ACT", lambda e, tc=tc, ps=ps: e.copy(
                out=xs_t[:, 4 + tc * 512: 4 + (tc + 1) * 512], in_=ps[:]),
                reads=[pk], writes=[kxs], tag=f"p2e{sub}_{ec}_{tc}")
        zsl = 2 + (ec % 2) if False else sl
        for c in range(NDC):
            sch.op("SP", lambda q, c=c, ec=ec, sl=sl: q.dma_start(
                out=winsl[:, (sl * 8 + c) * 128:(sl * 8 + c + 1) * 128],
                in_=dch(winb_scr)[c][:, 2048 + ec * 128: 2048 + (ec + 1) * 128]),
                reads=[("wslf", sl)], writes=[("wsl", sl, c)],
                tag=f"p2zw{sub}_{ec}_{c}")
        for tc in range(2):
            ps = ps_a if tc == 0 else ps_b
            pk = ("psab", tc)
            for c in range(NDC):
                sch.op("PE", lambda t, c=c, tc=tc, sl=sl, ps=ps: t.matmul(
                    ps[:], winsl[:, (sl * 8 + c) * 128:(sl * 8 + c + 1) * 128],
                    usub[:, c * SUB + tc * 512: c * SUB + (tc + 1) * 512],
                    start=(c == 0), stop=(c == NDC - 1)),
                    reads=[("wsl", sl, c), ("u", c)], writes=[pk],
                    tag=f"p2zm{sub}_{ec}_{tc}_{c}")
            sch.op("ACT", lambda e, tc=tc, ps=ps: e.activation(
                out=sz_t[:, tc * 512:(tc + 1) * 512], in_=ps[:],
                func=AF.Silu),
                reads=[pk], writes=[ksz], tag=f"p2z{sub}_{ec}_{tc}")
        sch.op("DVE", lambda v, ec=ec: v.tensor_copy(
            out=xs_t[:, 1:4], in_=tail2[:, ec * 4:ec * 4 + 3]),
            reads=[("tl2",)], writes=[kxs], short=True,
            tag=f"p2t{sub}_{ec}")
        sch.op("DVE", lambda v, ec=ec: v.tensor_scalar(
            out=xc_t[:], in0=xs_t[:, 1:1 + SUB],
            scalar1=cw_sb[:, ec * 4:ec * 4 + 1], scalar2=None, op0=ALU.mult),
            reads=[kxs, ("cw",)], writes=[kxc], tag=f"c20{sub}_{ec}")
        for j in range(1, 4):
            sch.op("DVE", lambda v, j=j, ec=ec: v.scalar_tensor_tensor(
                out=xc_t[:], in0=xs_t[:, 1 + j:1 + j + SUB],
                scalar=cw_sb[:, ec * 4 + j:ec * 4 + j + 1], in1=xc_t[:],
                op0=ALU.mult, op1=ALU.add),
                reads=[kxs, ("cw",), kxc], writes=[kxc],
                tag=f"c2{j}{sub}_{ec}")
        sch.op("DVE", lambda v, ec=ec: v.tensor_copy(
            out=tail2[:, ec * 4:ec * 4 + 3], in_=xs_t[:, 1 + SUB:4 + SUB]),
            reads=[kxs], writes=[("tl2",)], short=True,
            tag=f"p2ts{sub}_{ec}")
        sch.op("ACT", lambda e, ec=ec: e.activation(
            out=xc_t[:], in_=xc_t[:], func=AF.Silu,
            bias=cb_sb[:, ec:ec + 1]),
            reads=[kxc, ("cvb",)], writes=[kxc], strict=True,
            tag=f"c2s{sub}_{ec}")
        for half in range(2):
            sch.op("PE", lambda t, ec=ec, half=half: t.matmul(
                ps_dl[:, half * 512:(half + 1) * 512],
                wdt_sb[:, ec * 128:(ec + 1) * 128],
                dt_bf[:, half * 512: (half + 1) * 512],
                start=True, stop=True),
                reads=[("dt",), ("wdt",)], writes=[("psdl",)],
                tag=f"dl{sub}_{ec}_{half}")
        sch.op("ACT", lambda e, ec=ec: e.activation(
            out=exp_t[:], in_=ps_dl[:], func=AF.Exp,
            bias=bdt_sb[:, ec:ec + 1]),
            reads=[("psdl",), ("bdt",)], writes=[kexp],
            tag=f"de{sub}_{ec}")
        sch.op("ACT", lambda e: e.activation(
            out=delta_t[:], in_=exp_t[:], func=AF.Ln, bias=1.0),
            reads=[kexp], writes=[kdlt], strict=True,
            tag=f"dn{sub}_{ec}")
        sch.op("DVE", lambda v, ec=ec: v.tensor_scalar(
            out=dpxc_t[:], in0=xc_t[:], scalar1=dp_sb[:, ec:ec + 1],
            scalar2=None, op0=ALU.mult),
            reads=[kxc, ("dp",)], writes=[kxcb], tag=f"dp{sub}_{ec}")
        sch.op("DVE", lambda v: v.tensor_tensor(
            out=xcbf_t[:], in0=xc_t[:], in1=delta_t[:], op=ALU.mult),
            reads=[kxc, kdlt], writes=[kxcf], strict=True,
            tag=f"dx{sub}_{ec}")

    def _p2_nloop(sub, t0, ec, sl, par, xc_t, dpxc_t, delta_t, sz_t, xcbf_t, kxc, kxcb, kdlt, ksz, kxcf):
        sch.op("PE", lambda t: t.matmul(
            ps_y[:, 0:512], cbb[:, 0:128], dpxc_t[:, 0:512],
            start=True, stop=False),
            reads=[kxcb, ("cb",)], writes=[("psy",)],
            tag=f"yi0{sub}_{ec}")
        sch.op("PE", lambda t: t.matmul(
            ps_y[:, 512:1024], cbb[:, 0:128], dpxc_t[:, 512:1024],
            start=True, stop=False),
            reads=[kxcb, ("cb",)], writes=[("psy",)],
            tag=f"yi1{sub}_{ec}")
        hsrc, hkey = (hinit, ("hini",)) if sub == 0 else (hstate, ("hst",))

        def em_brep(n):
            bsl = n % 2
            sch.op("SP", lambda q, n=n, t0=t0, bsl=bsl: q.dma_start(
                out=brep[:, bsl * SUB:(bsl + 1) * SUB],
                in_=bc_scr[n:n + 1, t0:t0 + SUB].broadcast_to((128, SUB))),
                reads=[("brepf", bsl), ("bcscr", sub)], writes=[("brep", bsl)],
                tag=f"bl{sub}_{ec}_{n}")
            sch.op("SP", lambda q, n=n, t0=t0, bsl=bsl: q.dma_start(
                out=crep[:, bsl * SUB:(bsl + 1) * SUB],
                in_=bc_scr[16 + n:17 + n, t0:t0 + SUB].broadcast_to((128, SUB))),
                reads=[("crepf", bsl), ("bcscr", sub)], writes=[("crep", bsl)],
                tag=f"cl{sub}_{ec}_{n}")

        def em_dA(n):
            asl = n % 2
            sch.op("ACT", lambda e, ec=ec, n=n, asl=asl: e.activation(
                out=dA_t[:, asl * SUB:(asl + 1) * SUB], in_=delta_t[:],
                func=AF.Exp,
                scale=A_sb[:, ec * 16 + n: ec * 16 + n + 1]),
                reads=[kdlt, ("A",)], writes=[("dA", asl)],
                tag=f"da{sub}_{ec}_{n}")

        def em_dBx(n):
            bsl = n % 2
            sch.op("DVE", lambda v, bsl=bsl: v.tensor_tensor(
                out=dBx_t[:, bsl * SUB:(bsl + 1) * SUB], in0=xcbf_t[:],
                in1=brep[:, bsl * SUB:(bsl + 1) * SUB], op=ALU.mult),
                reads=[kxcf, ("brep", bsl)], writes=[("dBx", bsl), ("brepf", bsl)],
                tag=f"db{sub}_{ec}_{n}")

        em_brep(0)
        em_brep(1)
        em_dA(0)
        em_dA(1)
        em_dBx(0)
        for n in range(16):
            asl = n % 2
            hsl = n % 2
            sch.op("DVE", lambda v, ec=ec, n=n, hsrc=hsrc, asl=asl, hsl=hsl:
                   v.tensor_tensor_scan(
                out=h_t[:, hsl * SUB:(hsl + 1) * SUB],
                data0=dA_t[:, asl * SUB:(asl + 1) * SUB],
                data1=dBx_t[:, hsl * SUB:(hsl + 1) * SUB],
                initial=hsrc[:, ec * 16 + n: ec * 16 + n + 1],
                op0=ALU.mult, op1=ALU.add),
                reads=[("dA", asl), ("dBx", hsl), hkey], writes=[("h", hsl)],
                tag=f"sc{sub}_{ec}_{n}")
            if n + 2 < 16:
                em_dA(n + 2)
            sch.op("ACT", lambda e, ec=ec, n=n, hsl=hsl: e.copy(
                out=hstate[:, ec * 16 + n: ec * 16 + n + 1],
                in_=h_t[:, hsl * SUB + SUB - 1: hsl * SUB + SUB]),
                reads=[("h", hsl)], writes=[("hst",)], short=True,
                tag=f"hs{sub}_{ec}_{n}")
            if n + 1 < 16:
                em_dBx(n + 1)
            sch.op("POOL", lambda g, hsl=hsl: g.tensor_tensor(
                out=hc_t[:, hsl * SUB:(hsl + 1) * SUB],
                in0=h_t[:, hsl * SUB:(hsl + 1) * SUB],
                in1=crep[:, hsl * SUB:(hsl + 1) * SUB], op=ALU.mult),
                reads=[("h", hsl), ("crep", hsl)],
                writes=[("hc", hsl), ("crepf", hsl)],
                tag=f"hc{sub}_{ec}_{n}")
            for half in range(2):
                sch.op("PE", lambda t, hsl=hsl, half=half, n=n: t.matmul(
                    ps_y[:, half * 512:(half + 1) * 512], cbb[:, 0:128],
                    hc_t[:, hsl * SUB + half * 512: hsl * SUB + (half + 1) * 512],
                    start=False, stop=(n == 15)),
                    reads=[("hc", hsl), ("cb",)], writes=[("psy",)],
                    tag=f"ya{sub}_{ec}_{n}_{half}")
            if n + 2 < 16:
                em_brep(n + 2)
        for half in range(2):
            sch.op("DVE", lambda v, ec=ec, half=half: v.tensor_tensor(
                out=ygsub[:, ec * SUB + half * 512: ec * SUB + (half + 1) * 512],
                in0=ps_y[:, half * 512:(half + 1) * 512],
                in1=sz_t[:, half * 512:(half + 1) * 512], op=ALU.mult),
                reads=[("psy",), ksz], writes=[("yg", ec)],
                tag=f"yg{sub}_{ec}_{half}")

    sch.op("DVE", lambda v: v.memset(tail0[:], 0.0), writes=[("tl0",)], short=True)
    sch.op("DVE", lambda v: v.memset(tail2[:], 0.0), writes=[("tl2",)], short=True)

    for c in range(NDC):
        sch.op("SP", lambda q, c=c: q.dma_start(out=xld[:], in_=dch(xin_d)[c]),
               reads=[("xldf", 0), ("xldf", 1), kin(c)],
               writes=[("xld", 0), ("xld", 1)], tag=f"rx{c}")
        sch.op("ACT", lambda e: e.activation(out=sqb[:], in_=xld[:], func=AF.Square),
               reads=[("xld", 0), ("xld", 1)],
               writes=[("sqb",), ("xldf", 0), ("xldf", 1)], tag=f"rs{c}")
        for tt in range(16):
            sch.op("PE", lambda t, c=c, tt=tt: t.matmul(
                ps_a[:, tt * 8 + c: tt * 8 + c + 1],
                sqb[:, tt * 128:(tt + 1) * 128], cbb[:, 128:129],
                start=True, stop=True),
                reads=[("sqb",), ("cb",)], writes=[("psab", 0)],
                tag=f"rm{c}_{tt}")
    sch.op("DVE", lambda v: v.tensor_reduce(
        out=statc[:, 0:16], in_=ps_a[:, 0:128].rearrange("p (tt c) -> p tt c", c=8),
        op=ALU.add, axis=AX.X),
        reads=[("psab", 0)], writes=[("ms",)], short=True, tag="rred")
    sch.op("DVE", lambda v: v.tensor_scalar(
        out=statc[:, 0:16], in0=statc[:, 0:16], scalar1=1.0 / D, scalar2=None,
        op0=ALU.mult),
        reads=[("ms",)], writes=[("ms",)], short=True, strict=True, tag="rdiv")
    sch.op("ACT", lambda e: e.activation(
        out=statc[:, 16:32], in_=statc[:, 0:16], func=AF.Ln, bias=cf[:, 193:194]),
        reads=[("ms",), ("cf",)], writes=[("lnv",)], short=True, strict=True,
        tag="rln")
    sch.op("ACT", lambda e: e.activation(
        out=statc[:, 0:16], in_=statc[:, 16:32], func=AF.Exp, scale=-0.5),
        reads=[("lnv",)], writes=[("rstd",)], short=True, strict=True, tag="rex")
    sch.op("PE", lambda t: t.transpose(ps_a[0:16, 0:128], statc[:, 0:16],
                                       cf[:, 0:128]),
           reads=[("rstd",), ("cf",)], writes=[("psab", 0)], tag="rtr")
    sch.op("DVE", lambda v: v.tensor_copy(out=statr[:], in_=ps_a[0:16, 0:128]),
           reads=[("psab", 0)], writes=[("strow",)], short=True, tag="rcp")
    sch.op("SP", lambda q: q.dma_start(
        out=st_scr.rearrange("r (a b) -> r a b", b=128)[0], in_=statr[:]),
        reads=[("strow",)], writes=[("stscr",)], tag="rsd")

    for sub in range(NSUB):
        t0 = sub * SUB
        sch.op("SP", lambda q, t0=t0: q.dma_start(
            out=rstd_rep[:], in_=st_scr[0:1, t0:t0 + SUB].broadcast_to((128, SUB))),
            reads=[("stscr",), ("rrepf",)], writes=[("rrep",)], tag=f"rsb{sub}")
        for c in range(NDC):
            usl = c % 2
            sch.op("SP", lambda q, c=c, t0=t0, usl=usl: q.dma_start(
                out=xld[:, usl * SUB:(usl + 1) * SUB],
                in_=dch(xin_d)[c][:, t0:t0 + SUB]),
                reads=[("xldf", usl), kin(c)], writes=[("xld", usl)],
                tag=f"ux{sub}_{c}")
            sch.op("DVE", lambda v, c=c, t0=t0, usl=usl: v.tensor_tensor(
                out=usub[:, c * SUB:(c + 1) * SUB],
                in0=xld[:, usl * SUB:(usl + 1) * SUB],
                in1=rstd_rep[:, 0:SUB], op=ALU.mult),
                reads=[("xld", usl), ("rrep",)],
                writes=([("u", c), ("xldf", usl), ("sqb",)]
                        + ([("rrepf",)] if c == NDC - 1 else [])),
                tag=f"uu{sub}_{c}")
        for ec in range(NE):
            par = ec % 2
            _p0_body(sub, t0, ec, ec % 2, par, xs_v[par], xc_v[par],
                     dpxc_v[par], ("xs", par), ("xc", par), ("xcb", par))
            if ec >= 1:
                pp = (ec - 1) % 2
                _p0_dbc(sub, ec - 1, dpxc_v[pp], ("xcb", pp))
        _p0_dbc(sub, NE - 1, dpxc_v[(NE - 1) % 2], ("xcb", (NE - 1) % 2))
        sch.op("ACT", lambda e, t0=t0: e.copy(
            out=dt_bf[:, 0:SUB], in_=ps_dbc[0:64, :]),
            reads=[("psdbc",)], writes=[("dt",)],
            tag=f"dte{sub}")
        sch.op("DVE", lambda v: v.tensor_copy(
            out=bc_ev[:], in_=ps_dbc[64:96, :]),
            reads=[("psdbc",)], writes=[("bcev",)],
            short=True, tag=f"bce{sub}")
        sch.op("SP", lambda q, t0=t0: q.dma_start(
            out=bc_scr[:, t0:t0 + SUB], in_=bc_ev[:]),
            reads=[("bcev",)], writes=[("bcscr", sub)], tag=f"bcd{sub}")

        def _front(ec):
            par = ec % 2
            _p2_front(sub, t0, ec, par, par, xs_v[par], xc_v[par],
                      dpxc_v[par], delta_v[par], exp_v[par], sz_v[par],
                      xcbf_v[par],
                      ("xs", par), ("xc", par), ("xcb", par), ("dlt", par),
                      ("expt",), ("sz", par), ("xcf", par))
        _front(0)
        for ec in range(NE):
            if ec + 1 < NE:
                _front(ec + 1)
            par = ec % 2
            _p2_nloop(sub, t0, ec, par, par, xc_v[par], dpxc_v[par],
                      delta_v[par], sz_v[par], xcbf_v[par],
                      ("xc", par), ("xcb", par), ("dlt", par), ("sz", par),
                      ("xcf", par))
        for oc in range(NDC):
            for tc in range(2):
                ps = ps_a if tc == 0 else ps_b
                pk = ("psab", tc)
                for ec in range(NE):
                    sch.op("PE", lambda t, oc=oc, tc=tc, ec=ec, ps=ps: t.matmul(
                        ps[:], wout_sb[:, ec * 1024 + oc * 128:
                                       ec * 1024 + (oc + 1) * 128],
                        ygsub[:, ec * SUB + tc * 512: ec * SUB + (tc + 1) * 512],
                        start=(ec == 0), stop=(ec == NE - 1)),
                        reads=[("yg", ec), ("wo", ec)], writes=[pk],
                        tag=f"om{sub}_{oc}_{tc}_{ec}")
            osl = oc % 2
            stg = xout_t if osl == 0 else exp_t
            stk_keys = [("xot", 0)] if osl == 0 else [("xot", 1), ("expt",)]
            sch.op("SP", lambda q, oc=oc, t0=t0, osl=osl: q.dma_start(
                out=xld[:, osl * SUB:(osl + 1) * SUB],
                in_=dch(xin_d)[oc][:, t0:t0 + SUB]),
                reads=[("xldf", osl), kin(oc)], writes=[("xld", osl)],
                tag=f"ox{sub}_{oc}")
            for tc in range(2):
                ps = ps_a if tc == 0 else ps_b
                pk = ("psab", tc)
                sch.op("DVE", lambda v, tc=tc, ps=ps, osl=osl, stg=stg:
                       v.scalar_tensor_tensor(
                    out=stg[:, tc * 512:(tc + 1) * 512],
                    in0=xld[:, osl * SUB + tc * 512: osl * SUB + (tc + 1) * 512],
                    scalar=2.0,
                    in1=ps[:], op0=ALU.mult, op1=ALU.add),
                    reads=[("xld", osl), pk], writes=list(stk_keys),
                    tag=f"oc{sub}_{oc}_{tc}")
            sch.op("SP", lambda q, oc=oc, t0=t0, stg=stg: q.dma_start(
                out=dch(xout_o)[oc][:, t0:t0 + SUB], in_=stg[:, 0:SUB]),
                reads=list(stk_keys), writes=[kout(oc), ("xldf", osl)],
                tag=f"od{sub}_{oc}")


def build_mamba():
    nc = bass.Bass("TRN2", target_bir_lowering=False, debug=False)
    xin_d = nc.dram_tensor("xin", [D, TOK], F32, kind="ExternalInput").ap()
    h0_d = nc.dram_tensor("h0", [128, 256], F32, kind="ExternalInput").ap()
    win_d = nc.dram_tensor("win", [D, 4096], F32, kind="ExternalInput").ap()
    rmsw_d = nc.dram_tensor("rmsw", [128, 8], F32, kind="ExternalInput").ap()
    convw_d = nc.dram_tensor("convw", [128, 64], F32, kind="ExternalInput").ap()
    convb_d = nc.dram_tensor("convb", [128, 16], F32, kind="ExternalInput").ap()
    wx_d = nc.dram_tensor("wx", [128, 16 * 96], F32, kind="ExternalInput").ap()
    wdt_d = nc.dram_tensor("wdt", [64, 2048], F32, kind="ExternalInput").ap()
    bdt_d = nc.dram_tensor("bdt", [128, 16], F32, kind="ExternalInput").ap()
    alog_d = nc.dram_tensor("alog", [128, 256], F32, kind="ExternalInput").ap()
    dp_d = nc.dram_tensor("dp", [128, 16], F32, kind="ExternalInput").ap()
    wout_d = nc.dram_tensor("woutm", [2048, 1024], F32, kind="ExternalInput").ap()
    consf_d = nc.dram_tensor("consf", [128, 258], F32, kind="ExternalInput").ap()
    consb_d = nc.dram_tensor("consb", [128, 130], BF16, kind="ExternalInput").ap()
    xout_o = nc.dram_tensor("xout", [D, TOK], F32, kind="ExternalOutput").ap()
    hT_o = nc.dram_tensor("hT", [128, 256], F32, kind="ExternalOutput").ap()
    winb_scr = nc.dram_tensor("winb_scr", [D, 4096], BF16, kind="Internal").ap()
    st_scr = nc.dram_tensor("st_scrM", [1, TOK], F32, kind="Internal").ap()
    bc_scr = nc.dram_tensor("bc_scr", [32, TOK], BF16, kind="Internal").ap()

    sch = Sched(nc)
    with contextlib.ExitStack() as stk:
        bufs = mamba_alloc(nc, stk)
        drams = dict(xin_d=xin_d, h0_d=h0_d, win_d=win_d, rmsw_d=rmsw_d, convw_d=convw_d, convb_d=convb_d, wx_d=wx_d, wdt_d=wdt_d, bdt_d=bdt_d, alog_d=alog_d, dp_d=dp_d, wout_d=wout_d, consf_d=consf_d, consb_d=consb_d, xout_o=xout_o, hT_o=hT_o, winb_scr=winb_scr, st_scr=st_scr, bc_scr=bc_scr)
        emit_mamba_prep(nc, sch, **drams, **bufs)
        sch.op("SP", lambda q: q.dma_start(out=bufs['hinit'][:], in_=h0_d),
               writes=[("hini",)])
        emit_mamba_slot(nc, sch, **drams, **bufs)
        sch.op("SP", lambda q: q.dma_start(out=hT_o, in_=bufs['hstate'][:]),
               reads=[("hst",)], writes=[("hTd",)], tag="hTd2")
    sch.emit()
    return nc




def emit_final(nc, sch, x1_d, moa_d, xm_d, lnw_d, lnb_d, consf_d, consb_d,
               out_o, st_scr):
    NDC = D // 128
    with contextlib.ExitStack() as stk:
        sb = lambda n, s, d=F32: stk.enter_context(nc.sbuf_tensor(_u(n), s, d))
        pt = lambda n, s, d=F32: stk.enter_context(nc.psum_tensor(_u(n), s, d))
        cf = sb("cf", [128, 258])
        cbb = sb("cbb", [128, 130], BF16)
        lnw = sb("lnw_s", [128, 8])
        lnb = sb("lnb_s", [128, 8])
        statc = sb("statc", [128, 3 * 16])
        statr = sb("statr", [32, 128])
        xc = sb("xcF", [128, TOK])
        xbf = sb("xbfF", [128, TOK], BF16)
        sqbf = sb("sqbfF", [128, TOK], BF16)
        mrep = sb("mrepF", [128, TOK])
        rrep = sb("rrepF", [128, TOK])
        ot = sb("ot", [128, TOK])
        ps_stat = pt("ps_stat", [128, 512])
        ps_tr = pt("ps_tr", [128, 512])
        sch.op("SP", lambda q: q.dma_start(out=cf[:], in_=consf_d), writes=[("cf",)])
        sch.op("SP", lambda q: q.dma_start(out=cbb[:], in_=consb_d), writes=[("cb",)])
        sch.op("SP", lambda q: q.dma_start(out=lnw[:], in_=lnw_d), writes=[("lnw",)])
        sch.op("SP", lambda q: q.dma_start(out=lnb[:], in_=lnb_d), writes=[("lnb",)])
        emit_ln_stats(sch, nc, xm_d, TOK, ps_stat, ps_tr, statc, statr, cbb, cf,
                      st_scr, None, "F", xc, xbf, sqbf)
        sch.op("SP", lambda q: q.dma_start(
            out=mrep[:], in_=st_scr[0:1, :].broadcast_to((128, TOK))),
            reads=[("F", "scr0")], writes=[("mrep",)])
        sch.op("SP", lambda q: q.dma_start(
            out=rrep[:], in_=st_scr[1:2, :].broadcast_to((128, TOK))),
            reads=[("F", "scr1")], writes=[("rrep",)])
        for c in range(NDC):
            sch.op("SP", lambda q, c=c: q.dma_start(out=xc[:], in_=dch(xm_d)[c]),
                   reads=[("F", "xcf")], writes=[("F", "xc")], tag=f"fx{c}")
            sch.op("DVE", lambda v: v.tensor_tensor(
                out=xc[:], in0=xc[:], in1=mrep[:], op=ALU.subtract),
                reads=[("F", "xc"), ("mrep",)], writes=[("F", "xc")], strict=True,
                tag=f"fs{c}")
            sch.op("DVE", lambda v: v.tensor_tensor(
                out=xc[:], in0=xc[:], in1=rrep[:], op=ALU.mult),
                reads=[("F", "xc"), ("rrep",)], writes=[("F", "xc")], strict=True,
                tag=f"fm{c}")
            sch.op("DVE", lambda v, c=c: v.tensor_scalar(
                out=xc[:], in0=xc[:], scalar1=lnw[:, c:c + 1],
                scalar2=lnb[:, c:c + 1], op0=ALU.mult, op1=ALU.add),
                reads=[("F", "xc"), ("lnw",), ("lnb",)], writes=[("F", "xc")],
                strict=True, tag=f"fw{c}")
            sch.op("SP", lambda q, c=c: q.dma_start(out=ot[:], in_=dch(x1_d)[c]),
                   reads=[("otf",)], writes=[("ot1",)], tag=f"f1{c}")
            sch.op("DVE", lambda v: v.tensor_tensor(
                out=xc[:], in0=xc[:], in1=ot[:], op=ALU.add),
                reads=[("F", "xc"), ("ot1",)], writes=[("F", "xc"), ("otf",)],
                strict=True, tag=f"fa{c}")
            sch.op("SP", lambda q, c=c: q.dma_start(out=ot[:], in_=dch(moa_d)[c]),
                   reads=[("otf",)], writes=[("ot2",)], tag=f"f2{c}")
            sch.op("DVE", lambda v: v.tensor_tensor(
                out=xc[:], in0=xc[:], in1=ot[:], op=ALU.add),
                reads=[("F", "xc"), ("ot2",)], writes=[("F", "xc"), ("otf",)],
                strict=True, tag=f"fb{c}")
            sch.op("SP", lambda q, c=c: q.dma_start(out=dch(out_o)[c], in_=xc[:]),
                   reads=[("F", "xc")], writes=[("odma",), ("F", "xcf")],
                   tag=f"fo{c}")

def build_final():
    nc = bass.Bass("TRN2", target_bir_lowering=False, debug=False)
    x1_d = nc.dram_tensor("x1", [D, TOK], F32, kind="ExternalInput").ap()
    moa_d = nc.dram_tensor("moa", [D, TOK], F32, kind="ExternalInput").ap()
    xm_d = nc.dram_tensor("xm", [D, TOK], F32, kind="ExternalInput").ap()
    lnw_d = nc.dram_tensor("lnw", [128, 8], F32, kind="ExternalInput").ap()
    lnb_d = nc.dram_tensor("lnb", [128, 8], F32, kind="ExternalInput").ap()
    consf_d = nc.dram_tensor("consf", [128, 258], F32, kind="ExternalInput").ap()
    consb_d = nc.dram_tensor("consb", [128, 130], BF16, kind="ExternalInput").ap()
    out_o = nc.dram_tensor("out", [D, TOK], F32, kind="ExternalOutput").ap()
    st_scr = nc.dram_tensor("st_scrF", [2, TOK], F32, kind="Internal").ap()
    sch = Sched(nc)
    emit_final(nc, sch, x1_d, moa_d, xm_d, lnw_d, lnb_d, consf_d, consb_d,
               out_o, st_scr)
    sch.emit()
    return nc




def build_fused():
    nc = bass.Bass("TRN2", target_bir_lowering=False, debug=False)
    DT = nc.dram_tensor
    keyx = DT("keyx", [D, KL], F32, kind="ExternalInput").ap()
    kbias = DT("kbias", [128, 32], F32, kind="ExternalInput").ap()
    band_d = DT("band", [128, 1152], BF16, kind="ExternalInput").ap()
    consf_d = DT("consf", [128, 258], F32, kind="ExternalInput").ap()
    consb_d = DT("consb", [128, 130], BF16, kind="ExternalInput").ap()
    identf_d = DT("identf", [128, 128], F32, kind="ExternalInput").ap()
    identb_d = DT("identb", [128, 128], BF16, kind="ExternalInput").ap()
    bp_d = DT("blockpat", [64, 1024], F32, kind="ExternalInput").ap()
    wq_d = DT("wq", [D, NH * HD], F32, kind="ExternalInput").ap()
    wkv_d = DT("wkv", [D, LD], F32, kind="ExternalInput").ap()
    wkup_d = DT("wkup", [LD, NH * HD], F32, kind="ExternalInput").ap()
    wvup_d = DT("wvup", [LD, NH * HD], F32, kind="ExternalInput").ap()
    wout_d = DT("wout", [NH * HD, D], F32, kind="ExternalInput").ap()
    wg_d = DT("wg", [352, 1024], F32, kind="ExternalInput").ap()
    wu_d = DT("wu", [352, 1024], F32, kind="ExternalInput").ap()
    wd_d = DT("wd", [128, 2816], F32, kind="ExternalInput").ap()
    wrT_d = DT("wrT", [D, 16], F32, kind="ExternalInput").ap()
    wmhT_d = DT("wmhT", [D, D], F32, kind="ExternalInput").ap()
    wmgT_d = DT("wmgT", [D, D], F32, kind="ExternalInput").ap()
    kT_d = DT("kT", [8 * 16 * 128, 256], F32, kind="ExternalInput").ap()
    vf_d = DT("vf", [8 * 16 * 256, 128], F32, kind="ExternalInput").ap()
    win_d = DT("win", [D, 4096], F32, kind="ExternalInput").ap()
    rmsw_d = DT("rmsw", [128, 8], F32, kind="ExternalInput").ap()
    convw_d = DT("convw", [128, 64], F32, kind="ExternalInput").ap()
    convb_d = DT("convb", [128, 16], F32, kind="ExternalInput").ap()
    wx_d = DT("wx", [128, 16 * 96], F32, kind="ExternalInput").ap()
    wdt_d = DT("wdt", [64, 2048], F32, kind="ExternalInput").ap()
    bdt_d = DT("bdt", [128, 16], F32, kind="ExternalInput").ap()
    alog_d = DT("alog", [128, 256], F32, kind="ExternalInput").ap()
    dp_d = DT("dp", [128, 16], F32, kind="ExternalInput").ap()
    woutm_d = DT("woutm", [2048, 1024], F32, kind="ExternalInput").ap()
    lnw_d = DT("lnw", [128, 8], F32, kind="ExternalInput").ap()
    lnb_d = DT("lnb", [128, 8], F32, kind="ExternalInput").ap()
    mskE_d = DT("mskE", [128, 4], F32, kind="ExternalInput").ap()
    out_o = DT("out", [D, TOK], F32, kind="ExternalOutput").ap()
    att_scr = DT("att_scr", [D, TOK], F32, kind="Internal").ap()
    statsA = DT("statsA", [2, KL], F32, kind="Internal").ap()
    statsA2 = DT("statsA2", [2, KL], F32, kind="Internal").ap()
    x1_scr = DT("x1_scr", [D, TOK], F32, kind="Internal").ap()
    hq_scr = DT("hq_scrF", [2816, TOK], BF16, kind="Internal").ap()
    moa_scr = DT("moa_scr", [D, TOK], F32, kind="Internal").ap()
    P0 = DT("P0", [D, TOK], F32, kind="Internal").ap()
    P1 = DT("P1", [D, TOK], F32, kind="Internal").ap()
    xm_scr = DT("xm_scr", [D, TOK], F32, kind="Internal").ap()
    st_scrD = DT("st_scrDf", [2, TOK], F32, kind="Internal").ap()
    w_scrD = DT("w_scrDf", [16, TOK], F32, kind="Internal").ap()
    winb_scr = DT("winb_scrF", [D, 4096], BF16, kind="Internal").ap()
    st_scrM = DT("st_scrMf", [1, TOK], F32, kind="Internal").ap()
    bc_scr = DT("bc_scrF", [32, TOK], BF16, kind="Internal").ap()
    st_scrF = DT("st_scrFf", [2, TOK], F32, kind="Internal").ap()
    h0_dummy = DT("h0_dummy", [128, 256], F32, kind="Internal").ap()
    hT_dummy = DT("hT_dummy", [128, 256], F32, kind="Internal").ap()
    wgT_ci = DT("wgT_ci", [1024, 352], BF16, kind="Internal")
    wuT_ci = DT("wuT_ci", [1024, 352], BF16, kind="Internal")
    wdT_ci = DT("wdT_ci", [2816, 128], BF16, kind="Internal")
    wgT_co = DT("wgT_co", [8 * 1024, 352], BF16, kind="Internal")
    wuT_co = DT("wuT_co", [8 * 1024, 352], BF16, kind="Internal")
    wdT_co = DT("wdT_co", [8 * 2816, 128], BF16, kind="Internal")
    hcc_i = DT("hcc_i", [128, 256], F32, kind="Internal")
    hcc_o = DT("hcc_o", [256, 256], F32, kind="Internal")

    sch = Sched(nc)
    emit_l0_attn(nc, sch, keyx, kbias, band_d, consf_d, consb_d, wq_d, wkv_d,
                 wkup_d, wvup_d, wout_d, att_scr, statsA2, statsA)
    sch.barrier()
    emit_wq_quant(nc, sch, wg_d, wu_d, wd_d, identb_d, wgT_ci.ap(), wuT_ci.ap(),
                  wdT_ci.ap())
    sch.barrier()
    for cin, cout, knm in [(wgT_ci, wgT_co, "ccg"), (wuT_ci, wuT_co, "ccu"),
                           (wdT_ci, wdT_co, "ccd")]:
        sch.op("POOL", lambda g, cin=cin, cout=cout: g.collective_compute(
            "AllGather", ALU.bypass, replica_groups=[list(range(8))],
            ins=[cin.ap().opt()], outs=[cout.ap().opt()]),
            writes=[(knm,)], tag=knm)
    sch.barrier()

    def wgT_ld(sb_wg, sb_wu):
        for c in range(8):
            for r in range(8):
                sch.op("SP", lambda q, c=c, r=r: q.dma_start(
                    out=sb_wg[:, c * 2816 + r * 352: c * 2816 + (r + 1) * 352],
                    in_=wgT_co.ap()[r * 1024 + c * 128: r * 1024 + (c + 1) * 128, :]),
                    reads=[("ccg",)], writes=[("wg", c)], tag=f"fwg{c}_{r}")
                sch.op("SP", lambda q, c=c, r=r: q.dma_start(
                    out=sb_wu[:, c * 2816 + r * 352: c * 2816 + (r + 1) * 352],
                    in_=wuT_co.ap()[r * 1024 + c * 128: r * 1024 + (c + 1) * 128, :]),
                    reads=[("ccu",)], writes=[("wu", c)], tag=f"fwu{c}_{r}")

    def wdT_ld(sb_wd):
        for i in range(22):
            for r in range(8):
                sch.op("SP", lambda q, i=i, r=r: q.dma_start(
                    out=sb_wd[:, i * 1024 + r * 128: i * 1024 + (r + 1) * 128],
                    in_=wdT_co.ap()[r * 2816 + i * 128: r * 2816 + (i + 1) * 128, :]),
                    reads=[("ccd",)], writes=[("wd", i)], tag=f"fwd{i}_{r}")

    emit_l0_ffn(nc, sch, keyx[:, TOK:KL], att_scr, statsA[:, TOK:KL], wgT_ld,
                wdT_ld, identf_d, identb_d, bp_d, x1_scr, hq_scr)
    sch.barrier()
    emit_moa(nc, sch, x1_scr, wrT_d, wmhT_d, wmgT_d, kT_d, vf_d, consf_d,
             consb_d, moa_scr, P0, st_scrD, w_scrD)
    sch.barrier()
    with contextlib.ExitStack() as stk:
        bufs = mamba_alloc(nc, stk)
        msk = stk.enter_context(nc.sbuf_tensor(_u("mskE_s"), [128, 4], F32))
        sch.op("SP", lambda q: q.dma_start(out=msk[:], in_=mskE_d),
               writes=[("msk",)], tag="mskld")
        drams = dict(xin_d=P0, h0_d=h0_dummy, win_d=win_d, rmsw_d=rmsw_d,
                     convw_d=convw_d, convb_d=convb_d, wx_d=wx_d, wdt_d=wdt_d,
                     bdt_d=bdt_d, alog_d=alog_d, dp_d=dp_d, wout_d=woutm_d,
                     consf_d=consf_d, consb_d=consb_d, xout_o=P1, hT_o=hT_dummy,
                     winb_scr=winb_scr, st_scr=st_scrM, bc_scr=bc_scr)
        emit_mamba_prep(nc, sch, **drams, **bufs)
        xld = bufs["xld"]
        hinit = bufs["hinit"]
        hstate = bufs["hstate"]
        for slot in range(7):
            xin = P0 if slot % 2 == 0 else P1
            xout = P1 if slot % 2 == 0 else P0
            kin = ((lambda c: ("xn2o", c)) if slot == 0 else
                   (lambda c, s=slot: ("PB", s % 2, c)))
            kout = lambda c, s=slot: ("PB", (s + 1) % 2, c)
            if slot == 0:
                sch.op("DVE", lambda v: v.memset(hinit[:], 0.0),
                       writes=[("hini",)], short=True, tag="hz0")
            else:
                if slot == 1:
                    for c in range(8):
                        sch.op("SP", lambda q, c=c: q.dma_start(
                            out=xld[:], in_=dch(P1)[c]),
                            reads=[("xldf", 0), ("xldf", 1), ("PB", 1, c)],
                            writes=[("xld", 0), ("xld", 1)],
                            tag=f"bl1{c}")
                        sch.op("DVE", lambda v: v.tensor_scalar(
                            out=xld[:], in0=xld[:], scalar1=msk[:, 0:1],
                            scalar2=None, op0=ALU.mult),
                            reads=[("xld", 0), ("xld", 1), ("msk",)],
                            writes=[("xld", 0), ("xld", 1)],
                            strict=True, tag=f"bl2{c}")
                        sch.op("SP", lambda q, c=c: q.dma_start(
                            out=bufs["xout_t"][:, 0:SUB_G],
                            in_=dch(P0)[c][:, 0:SUB_G]),
                            reads=[("blx",), ("xn2o", c)],
                            writes=[("blx0",), ("xot",)], tag=f"bl3{c}")
                        sch.op("SP", lambda q, c=c: q.dma_start(
                            out=bufs["sz_t"][:, 0:SUB_G],
                            in_=dch(P0)[c][:, SUB_G:TOK]),
                            reads=[("blx",), ("xn2o", c)],
                            writes=[("blx1",), ("sz",)], tag=f"bl4{c}")
                        sch.op("DVE", lambda v: v.scalar_tensor_tensor(
                            out=xld[:, 0:SUB_G], in0=bufs["xout_t"][:],
                            scalar=msk[:, 1:2], in1=xld[:, 0:SUB_G],
                            op0=ALU.mult, op1=ALU.add),
                            reads=[("blx0",), ("msk",), ("xld", 0), ("xld", 1)],
                            writes=[("xld", 0), ("xld", 1)], strict=True, tag=f"bl5{c}")
                        sch.op("DVE", lambda v: v.scalar_tensor_tensor(
                            out=xld[:, SUB_G:TOK], in0=bufs["sz_t"][:, 0:SUB_G],
                            scalar=msk[:, 1:2], in1=xld[:, SUB_G:TOK],
                            op0=ALU.mult, op1=ALU.add),
                            reads=[("blx1",), ("msk",), ("xld", 0), ("xld", 1)],
                            writes=[("xld", 0), ("xld", 1), ("blx",)], strict=True, tag=f"bl6{c}")
                        sch.op("SP", lambda q, c=c: q.dma_start(
                            out=dch(P1)[c], in_=xld[:]),
                            reads=[("xld", 0), ("xld", 1)],
                            writes=[("xldf", 0), ("xldf", 1), ("PB", 1, c)],
                            tag=f"bl7{c}")
                sch.op("SP", lambda q: q.dma_start(out=hcc_i.ap(), in_=hstate[:]),
                       reads=[("hst",)], writes=[("hcci",)], tag=f"hx1_{slot}")
                sch.op("POOL", lambda g: g.collective_compute(
                    "AllGather", ALU.bypass,
                    replica_groups=[[0, 1], [2, 3], [4, 5], [6, 7]],
                    ins=[hcc_i.ap().opt()], outs=[hcc_o.ap().opt()]),
                    reads=[("hcci",)], writes=[("hcco",)], tag=f"hx2_{slot}")
                sch.op("SP", lambda q: q.dma_start(out=hinit[:],
                                                   in_=hcc_o.ap()[0:128, :]),
                       reads=[("hcco",)], writes=[("hini",)], tag=f"hx3_{slot}")
                sch.op("DVE", lambda v: v.tensor_scalar(
                    out=hinit[:], in0=hinit[:], scalar1=msk[:, 2:3],
                    scalar2=None, op0=ALU.mult),
                    reads=[("hini",), ("msk",)], writes=[("hini",)],
                    strict=True, tag=f"hx4_{slot}")
            dr2 = dict(drams)
            dr2["xin_d"] = xin
            dr2["xout_o"] = xout
            emit_mamba_slot(nc, sch, **dr2, **bufs, st=f"s{slot}",
                            kin=kin, kout=kout)
        sch.barrier()
        for c in range(8):
            sch.op("SP", lambda q, c=c: q.dma_start(out=xld[:], in_=dch(P1)[c]),
                   reads=[("xldf", 0), ("xldf", 1), ("PB", 1, c)],
                            writes=[("xld", 0), ("xld", 1)],
                   tag=f"xb1{c}")
            sch.op("DVE", lambda v: v.tensor_scalar(
                out=xld[:], in0=xld[:], scalar1=msk[:, 1:2], scalar2=None,
                op0=ALU.mult),
                reads=[("xld", 0), ("xld", 1), ("msk",)],
                            writes=[("xld", 0), ("xld", 1)], strict=True,
                tag=f"xb2{c}")
            sch.op("SP", lambda q, c=c: q.dma_start(
                out=bufs["xout_t"][:, 0:SUB_G], in_=dch(P0)[c][:, 0:SUB_G]),
                reads=[("blx",), ("PB", 0, c)], writes=[("blx0",), ("xot",)],
                tag=f"xb3{c}")
            sch.op("SP", lambda q, c=c: q.dma_start(
                out=bufs["sz_t"][:, 0:SUB_G], in_=dch(P0)[c][:, SUB_G:TOK]),
                reads=[("blx",), ("PB", 0, c)], writes=[("blx1",), ("sz",)],
                tag=f"xb4{c}")
            sch.op("DVE", lambda v: v.scalar_tensor_tensor(
                out=xld[:, 0:SUB_G], in0=bufs["xout_t"][:], scalar=msk[:, 3:4],
                in1=xld[:, 0:SUB_G], op0=ALU.mult, op1=ALU.add),
                reads=[("blx0",), ("msk",), ("xld", 0), ("xld", 1)],
                writes=[("xld", 0), ("xld", 1)],
                strict=True, tag=f"xb5{c}")
            sch.op("DVE", lambda v: v.scalar_tensor_tensor(
                out=xld[:, SUB_G:TOK], in0=bufs["sz_t"][:, 0:SUB_G], scalar=msk[:, 3:4],
                in1=xld[:, SUB_G:TOK], op0=ALU.mult, op1=ALU.add),
                reads=[("blx1",), ("msk",), ("xld", 0), ("xld", 1)],
                writes=[("xld", 0), ("xld", 1), ("blx",)],
                strict=True, tag=f"xb6{c}")
            sch.op("SP", lambda q, c=c: q.dma_start(out=dch(xm_scr)[c], in_=xld[:]),
                   reads=[("xld", 0), ("xld", 1)], writes=[("xmscr",)], tag=f"xb7{c}")
    sch.barrier()
    emit_final(nc, sch, x1_scr, moa_scr, xm_scr, lnw_d, lnb_d, consf_d,
               consb_d, out_o, st_scrF)
    sch.emit()
    return nc


SUB_G = 1024


import ml_dtypes as _mld


def _consts():
    consf = np.zeros((128, 258), np.float32)
    consf[:, 0:128] = np.eye(128)
    consf[:, 128] = 1.0
    consf[0, 129:193] = 1.0
    consf[:, 193] = EPS
    consb = np.zeros((128, 130), np.float32)
    consb[:, 0:128] = np.eye(128)
    consb[:, 128] = 1.0
    return consf, consb.astype(_mld.bfloat16)


def _band():
    band = np.zeros((128, 1152), np.float32)
    cc = np.arange(1152)[None, :] - 512
    band[:, :] = (cc >= np.arange(128)[:, None]).astype(np.float32)
    return band.astype(_mld.bfloat16)


def _chunkv(v):
    return np.ascontiguousarray(np.asarray(v, np.float32).reshape(-1, 128).T)


def prep_core_inputs(ins):
    consf, consb = _consts()
    band = _band()
    identf = np.eye(128, dtype=np.float32)
    identb = identf.astype(_mld.bfloat16)
    x = np.asarray(ins["L0_attn_x_bld"], np.float32)
    bp = np.zeros((64, 1024), np.float32)
    idx = np.arange(1024)
    bp[idx // 16, idx] = 1.0
    wq = np.asarray(ins["L0_attn_Wq"], np.float32).reshape(D, NH * HD)
    wkv = np.asarray(ins["L0_attn_Wkv_down"], np.float32)
    wkup = np.asarray(ins["L0_attn_Wk_up"], np.float32).reshape(LD, NH * HD)
    wvup = np.asarray(ins["L0_attn_Wv_up"], np.float32).reshape(LD, NH * HD)
    wout = np.asarray(ins["L0_attn_Wout"], np.float32).reshape(NH * HD, D)
    wg = np.asarray(ins["L0_ffn_w_gate"], np.float32)
    wu = np.asarray(ins["L0_ffn_w_up"], np.float32)
    wd = np.asarray(ins["L0_ffn_w_down"], np.float32)
    wrT = np.ascontiguousarray(np.asarray(ins["L1_ffn_w_router"], np.float32).T)
    wmhT = np.ascontiguousarray(np.asarray(ins["L1_ffn_w_mh"], np.float32).T)
    wmgT = np.ascontiguousarray(np.asarray(ins["L1_ffn_w_mg"], np.float32).T)
    kT = np.ascontiguousarray(np.transpose(
        np.asarray(ins["L1_ffn_k_ffwd"], np.float32), (0, 1, 3, 2))
    ).reshape(8 * 16 * 128, 256)
    vf = np.ascontiguousarray(np.asarray(ins["L1_ffn_v_ffwd"], np.float32)
                              ).reshape(8 * 16 * 256, 128)
    win = np.asarray(ins["L1_attn_W_in"], np.float32)
    rmsw = np.ascontiguousarray(np.asarray(ins["L1_attn_rms_w"], np.float32)
                                .reshape(8, 128).T)
    convw = np.ascontiguousarray(np.asarray(ins["L1_attn_conv_w"], np.float32)
                                 .reshape(16, 128, 4).transpose(1, 0, 2)
                                 .reshape(128, 64))
    convb = _chunkv(ins["L1_attn_conv_b"])
    wx = np.ascontiguousarray(np.asarray(ins["L1_attn_W_x"], np.float32)
                              .reshape(16, 128, 96).transpose(1, 0, 2)
                              .reshape(128, 16 * 96))
    wdt = np.asarray(ins["L1_attn_W_dt"], np.float32)
    bdt = _chunkv(ins["L1_attn_b_dt"])
    alog = np.ascontiguousarray(np.asarray(ins["L1_attn_A_log"], np.float32)
                                .reshape(16, 128, 16).transpose(1, 0, 2)
                                .reshape(128, 256))
    dp = _chunkv(ins["L1_attn_Dp"])
    woutm = np.asarray(ins["L1_attn_W_out"], np.float32)
    lnw = _chunkv(ins["L1_attn_ln_w"])
    lnb = _chunkv(ins["L1_attn_ln_b"])
    maps = []
    for c in range(NCORES):
        b, hh = c // 2, c % 2
        if hh == 0:
            keyx = _fm(np.roll(x[b], 2048, axis=0))
        else:
            keyx = _fm(x[b])
        kbias = np.zeros((128, 32), np.float32)
        if hh == 0:
            kbias[:, 0:16] = NEG
        mskE = np.zeros((128, 4), np.float32)
        mskE[:, 0] = 1.0 if hh == 0 else 0.0
        mskE[:, 1] = 0.0 if hh == 0 else 1.0
        mskE[:, 2] = 0.0 if hh == 0 else 1.0
        mskE[:, 3] = 1.0 if hh == 0 else 0.0
        maps.append(dict(
            keyx=keyx, kbias=kbias, band=band, consf=consf, consb=consb,
            identf=identf, identb=identb, blockpat=bp,
            wq=wq, wkv=wkv, wkup=wkup, wvup=wvup, wout=wout,
            wg=wg[c * 352:(c + 1) * 352], wu=wu[c * 352:(c + 1) * 352],
            wd=wd[c * 128:(c + 1) * 128],
            wrT=wrT, wmhT=wmhT, wmgT=wmgT, kT=kT, vf=vf,
            win=win, rmsw=rmsw, convw=convw, convb=convb, wx=wx, wdt=wdt,
            bdt=bdt, alog=alog, dp=dp, woutm=woutm, lnw=lnw, lnb=lnb,
            mskE=mskE))
    return maps


_NC_CACHE = {}


def kernel(**ins):
    if "fused" not in _NC_CACHE:
        _NC_CACHE["fused"] = build_fused()
    nc = _NC_CACHE["fused"]
    maps = prep_core_inputs(ins)
    res = run_bass_kernel_spmd(nc, maps, core_ids=list(range(NCORES)))
    out = np.empty((B, L, D), np.float32)
    for c in range(NCORES):
        b, hh = c // 2, c % 2
        out[b, hh * TOK:(hh + 1) * TOK] = res.results[c]["out"].T
    return out
